# version 26
# baseline (speedup 1.0000x reference)
import numpy as np
from contextlib import ExitStack

import concourse.bass as bass
import concourse.bacc as bacc
import concourse.mybir as mybir
from concourse.bass_utils import run_bass_kernel_spmd

F32 = mybir.dt.float32
BF16 = mybir.dt.bfloat16
I16 = mybir.dt.int16
U32 = mybir.dt.uint32
ALU = mybir.AluOpType
AF = mybir.ActivationFunctionType
AX = mybir.AxisListType

T, D, H, HDIM, E, FH = 4096, 1024, 16, 64, 8, 256
NS, SLAB, NSUB = 8, 512, 4
KSEL = 512
VOC, VOCP = 50176, 3136
ENGINES = ("sync", "tensor", "vector", "scalar", "gpsimd")

SLOT_MASKED = [False if i < 256 else ((i % 16) * 16 + (i // 16 - 16)) < 121
               for i in range(KSEL)]
assert sum(SLOT_MASKED) == 121


class Prog:
    def __init__(self, nc, ctx):
        self.nc, self.ctx = nc, ctx
        self.ops = {e: [] for e in ENGINES}
        self.sems, self.cnt = {}, {}
        self.lastw = {e: {} for e in ENGINES}

    def sem(self, name):
        self.sems[name] = self.ctx.enter_context(self.nc.semaphore(name))
        self.cnt[name] = 0

    def emit(self, eng, fn, inc=None, amt=1):
        if inc is not None:
            self.cnt[inc] += amt
            s = self.sems[inc]
            self.ops[eng].append(lambda e, fn=fn, s=s, a=amt: fn(e).then_inc(s, a))
            return self.cnt[inc]
        self.ops[eng].append(fn)
        return None

    def wait(self, eng, sem, val=None):
        v = self.cnt[sem] if val is None else val
        if v is None or v <= 0 or self.lastw[eng].get(sem, -1) >= v:
            return
        self.lastw[eng][sem] = v
        s = self.sems[sem]
        self.ops[eng].append(lambda e, s=s, v=v: e.wait_ge(s, v))

    def dma(self, eng, out_ap, in_ap, **kw):
        inc = {"scalar": "dA", "gpsimd": "dG"}.get(eng, "dS")
        return self.emit(eng, lambda e, o=out_ap, i=in_ap, kw=kw: e.dma_start(o, i, **kw),
                         inc=inc, amt=16)

    def dmat(self, out_ap, in_ap):
        return self.emit("sync", lambda e, o=out_ap, i=in_ap: e.dma_start_transpose(o, i),
                         inc="dS", amt=16)

    def drain(self, eng="vector"):
        self.ops[eng].append(lambda e: e.drain())


def build(nc):
    x_in = nc.dram_tensor("x_in", [T, D], F32, kind="ExternalInput")
    w_attn = nc.dram_tensor("w_attn", [D, 3 * D], F32, kind="ExternalInput")
    w_proj = nc.dram_tensor("w_proj", [D, D], F32, kind="ExternalInput")
    wg_in = nc.dram_tensor("wg_in", [E, D], F32, kind="ExternalInput")
    lns_in = nc.dram_tensor("lns_in", [E, D], F32, kind="ExternalInput")
    lnb_in = nc.dram_tensor("lnb_in", [E, D], F32, kind="ExternalInput")
    w1_in = nc.dram_tensor("w1_in", [E, D, FH], F32, kind="ExternalInput")
    b1_in = nc.dram_tensor("b1_in", [E, FH], F32, kind="ExternalInput")
    w2_in = nc.dram_tensor("w2_in", [E, FH, D], F32, kind="ExternalInput")
    b2_in = nc.dram_tensor("b2_in", [E, D], F32, kind="ExternalInput")
    y_out = nc.dram_tensor("y_out", [T, D], F32, kind="ExternalOutput")

    q_hbm = nc.dram_tensor("q_hbm", [8, 128, T], BF16)
    kv_hbm = nc.dram_tensor("kv_hbm", [8, 128, 2 * T], BF16)
    last_hbm = nc.dram_tensor("last_hbm", [16, T], F32)
    t1_hbm = nc.dram_tensor("t1_hbm", [2, 128], F32)
    idx_hbm = nc.dram_tensor("idx_hbm", [2, 128, 32], I16)
    nm_hbm = nc.dram_tensor("nm_hbm", [8, SLAB], BF16)
    lastp_hbm = nc.dram_tensor("lastp_hbm", [8, 16, T], F32)

    ctx = ExitStack()
    p = Prog(nc, ctx)
    for s in ("dA", "dS", "dG", "t", "v", "s", "g"):
        p.sem(s)

    def sb(name, shape, dt=F32):
        return ctx.enter_context(nc.sbuf_tensor(name, shape, dt))

    identf = sb("identf", [128, 128])
    eps_t = sb("eps_t", [128, 1])
    j2scr = sb("j2scr", [128, 1])
    expbias = sb("expbias", [128, 4])
    neg30row = sb("neg30row", [1, 128], BF16)
    ones_col = sb("ones_col", [128, 1], BF16)
    qlast = sb("qlast", [128, 8], BF16)
    blkd = sb("blkd", [128, 128], BF16)
    KT = sb("KT", [128, 8 * KSEL], BF16)
    VO = sb("VO", [128, 64 * 65], BF16)
    psum = [ctx.enter_context(nc.psum_tensor(f"pb{i}", [128, 512], F32)) for i in range(8)]
    bank_guard = {}

    def bank_wait(bk):
        g = bank_guard.get(bk)
        if g:
            p.wait("tensor", g[0], g[1])

    rowi = sb("rowi", [128, 128])
    coli = sb("coli", [128, 128])
    p.emit("gpsimd", lambda e: e.iota(coli[:, :], [[1, 128]], channel_multiplier=0,
                                      allow_small_or_imprecise_dtypes=True), inc="g")
    p.emit("gpsimd", lambda e: e.iota(rowi[:, :], [[0, 128]], channel_multiplier=1,
                                      allow_small_or_imprecise_dtypes=True), inc="g")
    p.wait("vector", "g")
    p.emit("vector", lambda e: e.tensor_tensor(identf[:, :], coli[:, :], rowi[:, :], ALU.is_equal), inc="v")
    v_ident = p.cnt["v"]
    p.emit("vector", lambda e: e.memset(eps_t[:, :], 1e-5))
    p.emit("vector", lambda e: e.memset(neg30row[:, :], -1.0e30))
    p.emit("vector", lambda e: e.memset(ones_col[:, :], 1.0))
    p.emit("vector", lambda e: e.memset(expbias[:, :], 0.0))
    p.emit("vector", lambda e: e.memset(blkd[:, :], 0.0), inc="v")
    p.wait("scalar", "g")
    p.wait("scalar", "v")
    p.dma("scalar", t1_hbm[0, :], rowi[:, 0:1])
    p.wait("scalar", "dA")
    p.dma("scalar", j2scr[:, :], bass.AP(t1_hbm, 0, [[1, 8], [16, 16]]), allow_slow_non_contiguous=True)
    p.wait("vector", "dA")
    p.emit("vector", lambda e: e.tensor_scalar(expbias[:, 2:3], j2scr[:, :], 120.5, -1.0e30, ALU.is_le, ALU.mult))
    p.drain()
    p.emit("vector", lambda e: e.tensor_scalar(expbias[:, 3:4], j2scr[:, :], 112.5, -1.0e30, ALU.is_le, ALU.mult), inc="v")
    p.drain()
    v_const = p.cnt["v"]
    _j2 = [16 * (r % 16) + r // 16 for r in range(128)]
    assert all((_j2[r] <= 120) == SLOT_MASKED[2 * 128 + r] for r in range(128))
    assert all((_j2[r] <= 112) == SLOT_MASKED[3 * 128 + r] for r in range(128))

    ffn_scope_names = None
    stage = sb("stage", [128, 1024])
    Wpb = sb("Wpb", [128, 8 * D], BF16)
    WgT = sb("WgT", [128, 64], BF16)
    W1b = sb("W1b", [128, E * 8 * FH], BF16)
    b1pT = sb("b1pT", [128, 16])
    b2b = sb("b2b", [8, D], BF16)
    seb = sb("seb", [128, 8])
    beb = sb("beb", [128, 8])
    bebb = sb("bebb", [128, 8], BF16)
    b1eb = sb("b1eb", [128, 2])

    p1_scope = ExitStack()

    def sb1(name, shape, dt=F32):
        return p1_scope.enter_context(nc.sbuf_tensor(name, shape, dt))

    xT = sb1("xT", [128, 8 * T], BF16)
    x0_scope = ExitStack()
    xs = x0_scope.enter_context(nc.sbuf_tensor("xs", [128, 2 * 2048], F32))
    xbb = x0_scope.enter_context(nc.sbuf_tensor("xbb", [128, 2 * 2048], BF16))

    slab_order = [7] + list(range(7))
    xT_done = {}
    cast_v = {}
    hparts = [(s, hh) for s in slab_order for hh in range(2)]
    for hi, (s, hh) in enumerate(hparts):
        buf = hi % 2
        if hi >= 2:
            p.wait("scalar", "v", cast_v[hi - 2])
        for u in range(2):
            uu = hh * 2 + u
            p.dma("scalar", xs[:, buf * 2048 + u * 1024: buf * 2048 + (u + 1) * 1024],
                  x_in[s * SLAB + uu * 128: s * SLAB + (uu + 1) * 128, :])
        dl = p.cnt["dA"]
        p.wait("vector", "dA", dl)
        if hi >= 2:
            p.wait("vector", "dS", xT_done[hi - 2])
        p.emit("vector", lambda e, b=buf: e.tensor_copy(
            xbb[:, b * 2048:(b + 1) * 2048], xs[:, b * 2048:(b + 1) * 2048]), inc="v")
        cast_v[hi] = p.cnt["v"]
        p.wait("sync", "v", cast_v[hi])
        for u in range(2):
            uu = hh * 2 + u
            dst = xT.rearrange("p (c x) -> p c x", c=8)[:, :, s * SLAB + uu * 128: s * SLAB + (uu + 1) * 128]
            srcb = xbb[:, buf * 2048 + u * 1024: buf * 2048 + (u + 1) * 1024]
            p.dmat(dst, srcb)
        xT_done[hi] = p.cnt["dS"]
    xT_all = p.cnt["dS"]
    x0_scope.close()

    Wst = sb1("Wst", [128, 1024])
    Wmb = sb1("Wmb", [128, 2 * 1024], BF16)
    qst = sb1("qst", [128, 2 * 512], BF16)

    wm_cast_by_buf = [0, 0]
    wm_used_t = [0, 0]

    def load_wm(m, buf):
        p.wait("scalar", "s", wm_cast_by_buf[0])
        p.dma("scalar", Wst[:, :],
              bass.AP(w_attn, m * 128, [[3072, 128], [128 * 3072, 8], [1, 128]]))
        d = p.cnt["dA"]
        p.wait("scalar", "dA", d)
        p.wait("scalar", "t", wm_used_t[buf])
        p.emit("scalar", lambda e, b=buf: e.copy(
            Wmb[:, b * 1024:(b + 1) * 1024], Wst[:, :]), inc="s")
        wm_cast_by_buf[0] = p.cnt["s"]
        return p.cnt["s"]

    p.wait("tensor", "dS", xT_done[1])
    qlast_wait = {0: load_wm(0, 0)}
    for m in range(8):
        buf = m % 2
        if m + 1 < 8:
            qlast_wait[m + 1] = load_wm(m + 1, (m + 1) % 2)
        p.wait("tensor", "s", qlast_wait[m])
        for k in range(8):
            lhs = Wmb[:, buf * 1024 + k * 128:(buf * 1024) + (k + 1) * 128]
            rhs = xT[:, k * T + 4095: k * T + 4096]
            p.emit("tensor", lambda e, l=lhs, r=rhs, m=m, st=(k == 0), sp=(k == 7):
                   e.matmul(psum[5][0:128, m:m + 1], l, r, start=st, stop=sp),
                   inc="t" if k == 7 else None)
        wm_used_t[buf] = p.cnt["t"]
    t_qlast = p.cnt["t"]
    p.wait("scalar", "t", t_qlast)
    p.emit("scalar", lambda e: e.copy(qlast[:, :], psum[5][0:128, 0:8]), inc="s")
    s_qlast = p.cnt["s"]
    bank_guard[5] = ("s", s_qlast)
    p.wait("vector", "s", s_qlast)
    for c in range(8):
        base = c * 16
        p.emit("vector", lambda e, c=c, base=base: e.tensor_copy(
            blkd[0:64, base + 2 * c:base + 2 * c + 1], qlast[0:64, c:c + 1]))
        p.emit("vector", lambda e, c=c, base=base: e.tensor_copy(
            blkd[64:128, base + 2 * c + 1:base + 2 * c + 2], qlast[64:128, c:c + 1]))
    p.emit("vector", lambda e: e.nop(), inc="v")
    blkd_v = p.cnt["v"]

    m_order = []
    for c in range(8):
        m_order += [8 + c, 16 + c]
    m_order += list(range(8))

    wp_state = {"prev_s": 0, "last_prep": 0}

    def wload(dst_ap, src_ap, wait_s=None, **kw):
        if wait_s:
            p.wait("sync", "s", wait_s)
        p.dma("sync", dst_ap, src_ap, **kw)
        return p.cnt["dS"]

    def wcast(dst_ap, src_ap, d):
        p.wait("scalar", "dS", d)
        p.emit("scalar", lambda e, o=dst_ap, i=src_ap: e.copy(o, i), inc="s")
        return p.cnt["s"]

    def p2b_wproj():
        for k in range(8):
            d = wload(stage[:, 0:1024], w_proj[k * 128:(k + 1) * 128, :], wait_s=wp_state["prev_s"])
            wp_state["prev_s"] = wcast(Wpb[:, k * D:(k + 1) * D], stage[:, 0:1024], d)
        wp_state["wp_all"] = wp_state["prev_s"]

    def p2b_misc():
        d = 0
        for k in range(8):
            d = wload(stage[:, 8 * k:8 * (k + 1)],
                      bass.AP(wg_in, k * 128, [[1, 128], [D, 8]]),
                      wait_s=(wp_state["prev_s"] if k == 0 else None),
                      allow_slow_non_contiguous=True)
        wp_state["wgt_s"] = wcast(WgT[:, :], stage[:, 0:64], d)
        d = wload(stage[0:8, 0:1024], b2_in[:, :], wait_s=wp_state["wgt_s"])
        wp_state["prev_s"] = wcast(b2b[:, :], stage[0:8, 0:1024], d)

    def p2b_expert(e_i):
        wload(seb[:, :], bass.AP(lns_in, e_i * D, [[1, 128], [128, 8]]),
              wait_s=wp_state["prev_s"], allow_slow_non_contiguous=True)
        wload(beb[:, :], bass.AP(lnb_in, e_i * D, [[1, 128], [128, 8]]),
              allow_slow_non_contiguous=True)
        wload(b1eb[:, :], bass.AP(b1_in, e_i * FH, [[1, 128], [128, 2]]),
              allow_slow_non_contiguous=True)
        d_se = p.cnt["dS"]
        for rr in range(2):
            for kk in range(4):
                k = rr * 4 + kk
                wload(stage[:, kk * FH:(kk + 1) * FH], w1_in[e_i, k * 128:(k + 1) * 128, :])
            d_w1 = p.cnt["dS"]
            p.wait("scalar", "dS", d_w1)
            p.emit("scalar", lambda e, e_i=e_i, rr=rr: e.copy(
                W1b[:, (e_i * 8 + rr * 4) * FH:(e_i * 8 + rr * 4 + 4) * FH],
                stage[:, 0:4 * FH]), inc="s")
            if rr == 0:
                p.wait("sync", "s", p.cnt["s"])
        p.wait("scalar", "dS", d_se)
        p.emit("scalar", lambda e: e.copy(bebb[:, :], beb[:, :]), inc="s")
        raw_s = p.cnt["s"]
        wp_state["prev_s"] = raw_s
        p.wait("tensor", "s", raw_s)
        for hc in range(2):
            bank_wait(5)
            for k in range(8):
                lhs = W1b[:, e_i * 8 * FH + k * FH + hc * 128: e_i * 8 * FH + k * FH + (hc + 1) * 128]
                p.emit("tensor",
                       lambda e, l=lhs, r=bebb[:, 0:8], k=k, st=(k == 0), sp=(k == 7):
                       e.matmul(psum[5][0:128, 0:1], l, r[:, k:k + 1], start=st, stop=sp),
                       inc="t" if k == 7 else None)
            t_b1 = p.cnt["t"]
            p.wait("vector", "t", t_b1)
            p.wait("vector", "dS", d_se)
            p.emit("vector", lambda e, e_i=e_i, hc=hc: e.tensor_tensor(
                b1pT[:, e_i * 2 + hc: e_i * 2 + hc + 1], psum[5][0:128, 0:1],
                b1eb[:, hc:hc + 1], ALU.add), inc="v")
            bank_guard[5] = ("v", p.cnt["v"])
        for k in range(8):
            p.emit("vector", lambda e, e_i=e_i, k=k: e.tensor_scalar(
                W1b[:, (e_i * 8 + k) * FH:(e_i * 8 + k + 1) * FH],
                W1b[:, (e_i * 8 + k) * FH:(e_i * 8 + k + 1) * FH],
                seb[:, k:k + 1], None, ALU.mult))
        p.emit("vector", lambda e: e.nop(), inc="v")
        wp_state["fold_v"] = p.cnt["v"]

    p2b_units = [p2b_wproj, p2b_misc] + [lambda e_i=e_i: p2b_expert(e_i) for e_i in range(E)]

    p2_scope = ExitStack()

    def sb2(name, shape, dt=F32):
        return p2_scope.enter_context(nc.sbuf_tensor(name, shape, dt))

    tk1 = [sb2(f"tk1_{g}", [128, 32], U32) for g in range(2)]
    tk2 = [sb2(f"tk2_{g}", [128, 32], U32) for g in range(2)]
    t1t = sb2("t1t", [128, 1])
    idx16 = sb2("idx16", [128, 32], I16)
    idxrep = sb2("idxrep", [128, 32], I16)
    kvG = sb2("kvG", [128, 2 * KSEL], BF16)
    VFt8 = sb2("VFt8", [128, 8 * KSEL], BF16)
    identb = sb2("identb", [128, 128], BF16)
    big30 = sb2("big30", [1, 64])
    scA = sb2("scA", [128, VOCP])
    sc2 = sb2("sc2", [128, VOCP])
    kvst_scope = ExitStack()
    kvstage = kvst_scope.enter_context(nc.sbuf_tensor("kvstage", [128, 8 * 1024], BF16))
    lastev = kvst_scope.enter_context(nc.sbuf_tensor("lastev", [16, 512], F32))
    kvL_holder = []

    sel = {"lastout": {}, "idx_ready": {}, "primed": False}

    def emit_sel_group(g):
        if not sel["primed"]:
            sel["primed"] = True
            p.wait("vector", "v", v_ident)
            p.emit("vector", lambda e: e.tensor_copy(identb[:, :], identf[:, :]))
            p.emit("vector", lambda e: e.memset(big30[:, :], 1.0e30), inc="v")
            sel["identb_v"] = p.cnt["v"]
            p.wait("sync", "v")
            p.dma("sync", bass.AP(t1_hbm, 128, [[1, 64]]), big30[0:1, :])
            sel["d_b30"] = p.cnt["dS"]
        if g == 0:
            p.emit("vector", lambda e: e.memset(scA[:, :], -1.0e30), inc="v")
            p.wait("sync", "v")
        else:
            p.wait("sync", "v", sel["ko_v"])
            p.wait("sync", "g", sel["tkA0_g"])
        p.wait("sync", "dA", sel["lastout"][g])
        p.dma("sync", scA[0:128:16, 0:VOCP],
              bass.AP(last_hbm, g * 8 * T, [[T, 8], [1, VOCP]]))
        p.dma("sync", scA[1:128:16, 0:T - VOCP],
              bass.AP(last_hbm, g * 8 * T + VOCP, [[T, 8], [1, T - VOCP]]))
        p.wait("sync", "dS", sel["d_b30"])
        p.dma("sync", scA[1:128:16, 896:960], bass.AP(t1_hbm, 128, [[0, 8], [1, 64]]))
        p.wait("gpsimd", "dS")
        p.emit("gpsimd", lambda e, g=g: e.topk(tk1[g][:, :], scA[:, :], tokens=8, vocab_size=VOC, k=256), inc="g")
        if g == 0:
            sel["tkA0_g"] = p.cnt["g"]
        p.wait("sync", "g")
        p.dma("sync", t1_hbm[g, :], tk1[g][:, 0:1].bitcast(F32))
        p.wait("sync", "dS")
        p.dma("sync", t1t[:, :], bass.AP(t1_hbm, g * 128, [[16, 8], [0, 16], [1, 1]]))

    def emit_sel_group_p2(g):
        p.wait("vector", "dS")
        p.emit("vector", lambda e: e.tensor_scalar(sc2[:, :], scA[:, :], t1t[:, :], -1.0e30, ALU.is_ge, ALU.mult))
        p.drain()
        p.emit("vector", lambda e: e.tensor_tensor(sc2[:, :], scA[:, :], sc2[:, :], ALU.add), inc="v")
        if g == 0:
            sel["ko_v"] = p.cnt["v"]
        p.wait("gpsimd", "v")
        p.emit("gpsimd", lambda e, g=g: e.topk(tk2[g][:, :], sc2[:, :], tokens=8, vocab_size=VOC, k=256), inc="g")

    def emit_sel_group_p3(g):
        p.wait("vector", "g")
        p.emit("vector", lambda e, g=g: e.tensor_copy(idx16[:, 0:16], tk1[g][:, 16:32]))
        p.emit("vector", lambda e, g=g: e.tensor_copy(idx16[:, 16:32], tk2[g][:, 16:32]), inc="v")
        p.wait("sync", "v")
        p.dma("sync", idx_hbm[g, :, :], idx16[:, :])
        sel["idx_ready"][g] = p.cnt["dS"]

    gat = {"kvload": {}, "gath_g": {}, "unp_v": {}}

    def emit_gathers(pairs):
        if not kvL_holder:
            kvst_scope.close()
            kvL_holder.append(p2_scope.enter_context(nc.sbuf_tensor("kvL", [128, 8192], BF16)))
        kvL = kvL_holder[0]

        def kv_load(c):
            p.dma("gpsimd", kvL[:, :], kv_hbm[c, :, :])
            gat["kvload"][c] = p.cnt["dG"]

        if pairs[0] == 0:
            p.wait("gpsimd", "dA", sel["lastout"][1])
            kv_load(0)
        for c in pairs:
            g = c // 4
            p.wait("sync", "dS", sel["idx_ready"][g])
            if c >= 1:
                p.wait("sync", "g", gat["gath_g"][c - 1])
            for half in range(2):
                p.dma("sync", idxrep[half * 64:(half + 1) * 64, :],
                      bass.AP(idx_hbm, g * 128 * 32 + ((2 * c + half) % 8) * 16 * 32,
                              [[0, 4], [32, 16], [1, 32]]))
            d_rep = p.cnt["dS"]
            p.wait("gpsimd", "dS", d_rep)
            p.wait("gpsimd", "dG", gat["kvload"][c])
            if c >= 1:
                p.wait("gpsimd", "v", gat["unp_v"][c - 1])
            p.emit("gpsimd", lambda e, c=c: e.ap_gather(
                kvG[:, :], kvL[:, :], idxrep[:, :],
                channels=128, num_elems=T, d=2, num_idxs=KSEL), inc="g")
            gat["gath_g"][c] = p.cnt["g"]
            if c + 1 < 8:
                kv_load(c + 1)
            p.wait("vector", "g", gat["gath_g"][c])
            kvG2 = kvG.rearrange("p (x two) -> p x two", two=2)
            p.emit("vector", lambda e, c=c, kvG2=kvG2: e.tensor_copy(KT[:, c * KSEL:(c + 1) * KSEL], kvG2[:, :, 0:1]))
            p.emit("vector", lambda e, c=c, kvG2=kvG2: e.tensor_copy(
                VFt8[:, c * KSEL:(c + 1) * KSEL], kvG2[:, :, 1:2]), inc="v")
            gat["unp_v"][c] = p.cnt["v"]
        sel["unp_v"] = gat["unp_v"]

    wm_cast = {}
    wm_cast[m_order[0]] = load_wm(m_order[0], 0)
    kv_out_d = {c: 0 for c in range(8)}
    qst_dma = [0, 0]
    kvst_dma = {}
    last_started = [False] * 8
    for mi, m in enumerate(m_order):
        buf = mi % 2
        if mi + 1 < 24:
            nm_ = m_order[mi + 1]
            wm_cast[nm_] = load_wm(nm_, (mi + 1) % 2)
        p.wait("tensor", "s", wm_cast[m])
        p.wait("tensor", "dS", xT_all)
        for hp in range(2):
            for k in range(8):
                lhs = Wmb[:, buf * 1024 + k * 128: buf * 1024 + (k + 1) * 128]
                for sl in range(4):
                    s4 = hp * 4 + sl
                    if k == 0:
                        g = bank_guard.get(sl)
                        if g:
                            p.wait("tensor", g[0], g[1])
                    rhs = xT[:, k * T + s4 * SLAB: k * T + (s4 + 1) * SLAB]
                    p.emit("tensor", lambda e, l=lhs, r=rhs, sl=sl, st=(k == 0), sp=(k == 7):
                           e.matmul(psum[sl][:, :], l, r, start=st, stop=sp),
                           inc="t" if (k == 7 and sl == 3) else None)
            t_mm = p.cnt["t"]
            for sl in range(4):
                s4 = hp * 4 + sl
                if m < 8:
                    qb = qst[:, (s4 % 2) * 512:(s4 % 2 + 1) * 512]
                    p.wait("scalar", "t", t_mm)
                    p.wait("scalar", "dA", qst_dma[s4 % 2])
                    ev = p.emit("scalar", lambda e, o=qb, i=psum[sl][:, :]: e.copy(o, i), inc="s")
                    p.wait("scalar", "s", ev)
                    p.dma("scalar", q_hbm[m, :, s4 * SLAB:(s4 + 1) * SLAB], qb)
                    qst_dma[s4 % 2] = p.cnt["dA"]
                    bank_guard[sl] = ("s", ev)
                else:
                    c = m - 8 if m < 16 else m - 16
                    which = 0 if m < 16 else 1
                    dst = kvstage[:, s4 * 1024:(s4 + 1) * 1024].rearrange(
                        "p (x two) -> p x two", two=2)[:, :, which:which + 1]
                    p.wait("vector", "t", t_mm)
                    p.wait("vector", "dA", kvst_dma.get(s4, 0))
                    p.emit("vector", lambda e, o=dst, i=psum[sl][:, :]: e.tensor_copy(o, i), inc="v")
                    bank_guard[sl] = ("v", p.cnt["v"])
        wm_used_t[buf] = p.cnt["t"]
        v_kv = p.cnt["v"]
        if 16 <= m < 24:
            c = m - 16
            p.wait("tensor", "v", blkd_v)
            p.wait("tensor", "v", v_kv)
            for s4 in range(8):
                lb = (4, 6, 7)[s4 // 3]
                pb = 32 * (s4 % 3)
                rhs = kvstage[:, s4 * 1024:(s4 + 1) * 1024].rearrange(
                    "p (x two) -> p x two", two=2)[:, :, 0:1]
                st0 = not last_started[s4]
                p.emit("tensor", lambda e, l=blkd[:, c * 16:(c + 1) * 16], r=rhs, lb=lb, pb=pb,
                       st=st0, sp=(c == 7):
                       e.matmul(psum[lb][pb:pb + 16, 0:512], l, r, start=st, stop=sp),
                       inc="t" if s4 == 7 else None)
                last_started[s4] = True
            t_last = p.cnt["t"]
            p.wait("scalar", "v", v_kv)
            for s4 in range(8):
                p.dma("scalar", kv_hbm[c, :, s4 * 2 * SLAB:(s4 + 1) * 2 * SLAB],
                      kvstage[:, s4 * 1024:(s4 + 1) * 1024])
                kvst_dma[s4] = p.cnt["dA"]
            kv_out_d[c] = p.cnt["dA"]
            if c in (3, 7):
                gsel = 0 if c == 3 else 1
                ro = gsel * 8
                p.wait("scalar", "t", t_last)
                for s4 in range(8):
                    lb = (4, 6, 7)[s4 // 3]
                    pb = 32 * (s4 % 3)
                    if s4 > 0:
                        p.wait("scalar", "dA", p.cnt["dA"])
                    p.emit("scalar", lambda e, lb=lb, pb=pb: e.copy(
                        lastev[0:16, 0:512], psum[lb][pb:pb + 16, 0:512]), inc="s")
                    p.wait("scalar", "s", p.cnt["s"])
                    p.dma("scalar", bass.AP(last_hbm, ro * T + s4 * 512, [[T, 8], [1, 512]]),
                          lastev[ro:ro + 8, 0:512])
                sel["lastout"][gsel] = p.cnt["dA"]
                if c == 7:
                    for bb in (4, 6, 7):
                        bank_guard[bb] = ("s", p.cnt["s"])
        if 1 <= mi <= 10:
            p2b_units[mi - 1]()
        if mi == 7:
            emit_sel_group(0)
        if mi == 10:
            emit_sel_group_p2(0)
        if mi == 13:
            emit_sel_group_p3(0)
        if mi == 15:
            emit_gathers(range(0, 4))
            emit_sel_group(1)
            emit_sel_group_p2(1)
            emit_sel_group_p3(1)
            emit_gathers(range(4, 8))
    p1_t = p.cnt["t"]
    kv_all_out2 = kv_out_d[7]
    wp_all = wp_state["wp_all"]
    wgt_v = wp_state["wgt_s"]
    ffnw_v = wp_state["fold_v"]
    ffnw_s = wp_state["prev_s"]

    unp_v = sel["unp_v"]
    for c in range(8):
        p.wait("tensor", "v", unp_v[c])
        p.wait("tensor", "v", sel["identb_v"])
        for hh in range(2):
            h = 2 * c + hh
            for kc in range(4):
                bk = 3 + (kc % 2) * 4
                bank_wait(bk)
                p.emit("tensor", lambda e, bk=bk, hh=hh, c=c,
                       i=VFt8[hh * 64:(hh + 1) * 64, c * KSEL + kc * 128:c * KSEL + (kc + 1) * 128]:
                       e.transpose(psum[bk].bitcast(BF16)[0:128, 0:64], i,
                                   identb[hh * 64:(hh + 1) * 64, hh * 64:(hh + 1) * 64]), inc="t")
                t_tr = p.cnt["t"]
                vo = VO[:, (h * 4 + kc) * 65:(h * 4 + kc) * 65 + 65]
                p.wait("scalar", "t", t_tr)
                p.emit("scalar", lambda e, o=vo, bk=bk: e.copy(o[:, 0:64], psum[bk].bitcast(BF16)[0:128, 0:64]), inc="s")
                p.emit("scalar", lambda e, o=vo: e.copy(o[:, 64:65], ones_col[:, :]), inc="s")
                bank_guard[bk] = ("s", p.cnt["s"])
    vo_s = p.cnt["s"]
    p2_scope.close()
    p1_scope.close()

    W2b = sb("W2b", [128, E * 2 * D], BF16)
    p.wait("sync", "g")
    w2_prev = [ffnw_s]
    for e_i in range(E):
        for hc in range(2):
            d = wload(stage[:, 0:1024], w2_in[e_i, hc * 128:(hc + 1) * 128, :], wait_s=w2_prev[0])
            w2_prev[0] = wcast(W2b[:, (e_i * 2 + hc) * D:(e_i * 2 + hc + 1) * D], stage[:, 0:1024], d)
    ffnw_s = w2_prev[0]

    qs = sb("qs", [128, 4096], BF16)
    xs3 = sb("xs3", [128, 2 * 1024])
    Eh = sb("Eh", [128, 2 * 2048], BF16)
    rc_t = sb("rc_t", [128, 1])
    projin = sb("projin", [128, 4 * 1024], BF16)
    projinT = sb("projinT", [128, 4096], BF16)
    xh1 = sb("xh1", [128, 4 * 1024])
    xh1b = sb("xh1b", [128, 2048], BF16)
    xh1T = sb("xh1T", [128, 4096], BF16)
    hb = sb("hb", [128, E * 1024], BF16)
    r1 = sb("r1", [128, 1024])
    outt = sb("outt", [128, 1024])
    lg = sb("lg", [128, 32])
    lgm = sb("lgm", [128, 4])
    ohf = sb("ohf", [128, 32])
    ohT = sb("ohT", [8, SLAB], BF16)
    ohTf = sb("ohTf", [8, 256], BF16)
    notT = sb("notT", [8, SLAB], BF16)
    nmflat = sb("nmflat", [1, 8 * SLAB], BF16)
    s_mu = sb("s_mu", [128, 1])
    s_sq = sb("s_sq", [128, 1])
    s_m2 = sb("s_m2", [128, 1])
    s_var = sb("s_var", [128, 1])
    s_sd = sb("s_sd", [128, 1])
    s_ri = sb("s_ri", [128, 1])

    def layer_norm(src_ap, n, t_guard):
        p.drain()
        p.emit("vector", lambda e: e.tensor_reduce(s_mu[:, :], src_ap, AX.X, ALU.add), inc="v")
        p.emit("scalar", lambda e, n=n: e.activation(Eh[:, 0:n], src_ap, AF.Square, accum_out=s_sq[:, :]), inc="s")
        s_c = p.cnt["s"]
        p.wait("vector", "s", s_c)
        p.drain()
        p.emit("vector", lambda e, n=n: e.tensor_scalar(s_mu[:, :], s_mu[:, :], 1.0 / n, None, ALU.mult))
        p.drain()
        p.emit("vector", lambda e, n=n: e.tensor_scalar(s_sq[:, :], s_sq[:, :], 1.0 / n, None, ALU.mult))
        p.drain()
        p.emit("vector", lambda e: e.tensor_tensor(s_m2[:, :], s_mu[:, :], s_mu[:, :], ALU.mult))
        p.drain()
        p.emit("vector", lambda e: e.tensor_tensor(s_var[:, :], s_sq[:, :], s_m2[:, :], ALU.subtract), inc="v")
        v_c = p.cnt["v"]
        p.wait("scalar", "v", v_c)
        p.emit("scalar", lambda e: e.activation(s_sd[:, :], s_var[:, :], AF.Sqrt, bias=eps_t[:, :]), inc="s")
        s_c2 = p.cnt["s"]
        p.wait("vector", "s", s_c2)
        p.emit("vector", lambda e: e.reciprocal(s_ri[:, :], s_sd[:, :]), inc="v")
        p.drain()
        return p.cnt["v"]

    store_d = [0, 0]
    qs_used_t = [0]
    q_d_cur = [0]
    xs3_used = {}
    eh_guard = [0, 0]
    av_s_prev = [0]
    projT_dS = [0]
    slab_end_t = [0]
    nm_guard_t = [0]
    nm_write_d = [0]
    gate_v = []
    xh1T_dS = {}

    def emit_q_loads(si):
        s0 = si * SLAB
        p.wait("scalar", "t", qs_used_t[0])
        for c in range(8):
            p.dma("scalar", qs[:, c * 512:(c + 1) * 512], q_hbm[c, :, s0:s0 + SLAB])
        q_d_cur[0] = p.cnt["dA"]

    def emit_scores(h):
        pair, off = h // 2, 64 * (h % 2)
        ebuf = h % 2
        p.wait("tensor", "dA", q_d_cur[0])
        p.wait("tensor", "s", vo_s)
        for kc in range(4):
            bk = kc % 2
            bank_wait(bk)
            lhs = KT[off:off + 64, pair * KSEL + kc * 128: pair * KSEL + (kc + 1) * 128]
            rhs = qs[off:off + 64, pair * 512:(pair + 1) * 512]
            p.emit("tensor", lambda e, bk=bk, l=lhs, r=rhs:
                   e.matmul(psum[bk][:, :], l, r, start=True, stop=True), inc="t")
            t_sc = p.cnt["t"]
            p.wait("scalar", "t", t_sc)
            p.wait("scalar", "t", eh_guard[ebuf])
            p.emit("scalar", lambda e, bk=bk, ebuf=ebuf, kc=kc: e.activation(
                Eh[:, ebuf * 2048 + kc * 512: ebuf * 2048 + (kc + 1) * 512],
                psum[bk][:, :], AF.Exp, bias=expbias[:, kc:kc + 1], scale=0.125), inc="s")
            bank_guard[bk] = ("s", p.cnt["s"])
        return p.cnt["s"]

    def emit_av(h, exp_done):
        ebuf = h % 2
        for u in range(NSUB):
            bk = 2 + (u % 2)
            bank_wait(bk)
            p.wait("tensor", "s", exp_done)
            for kc in range(4):
                lhs = Eh[:, ebuf * 2048 + kc * 512 + u * 128: ebuf * 2048 + kc * 512 + (u + 1) * 128]
                rhs = VO[:, (h * 4 + kc) * 65:(h * 4 + kc) * 65 + 65]
                p.emit("tensor", lambda e, bk=bk, l=lhs, r=rhs, st=(kc == 0), sp=(kc == 3):
                       e.matmul(psum[bk][0:128, 0:65], l, r, start=st, stop=sp),
                       inc="t" if kc == 3 else None)
            t_av = p.cnt["t"]
            if u == 3:
                eh_guard[ebuf] = t_av
            p.wait("vector", "t", t_av)
            p.wait("vector", "s", av_s_prev[0])
            p.emit("vector", lambda e, bk=bk: e.reciprocal(rc_t[:, :], psum[bk][0:128, 64:65]), inc="v")
            v_rc = p.cnt["v"]
            p.wait("scalar", "v", v_rc)
            p.emit("scalar", lambda e, bk=bk, u=u, h=h: e.activation(
                projin[:, u * 1024 + h * 64: u * 1024 + (h + 1) * 64],
                psum[bk][0:128, 0:64], AF.Copy, scale=rc_t[:, :]), inc="s")
            av_s_prev[0] = p.cnt["s"]
            bank_guard[bk] = ("s", p.cnt["s"])

    v_oh_map = {}
    notT_guard = {}

    def emit_gate_mm(si, u):
        p.wait("tensor", "dS", xh1T_dS[(si, u)])
        p.wait("tensor", "s", wgt_v)
        bank_wait(6)
        for c in range(8):
            lhs = xh1T[:, c * 512 + u * 128: c * 512 + (u + 1) * 128]
            p.emit("tensor", lambda e, l=lhs, c=c, st=(c == 0), sp=(c == 7):
                   e.matmul(psum[6][0:128, 0:8], l, WgT[:, c * 8:(c + 1) * 8], start=st, stop=sp),
                   inc="t" if c == 7 else None)
        t_g = p.cnt["t"]
        p.wait("scalar", "t", t_g)
        p.emit("scalar", lambda e, u=u: e.copy(lg[:, u * 8:(u + 1) * 8], psum[6][0:128, 0:8]), inc="s")
        s_lg = p.cnt["s"]
        bank_guard[6] = ("s", s_lg)
        p.wait("vector", "s", s_lg)
        p.emit("vector", lambda e, u=u: e.tensor_reduce(
            lgm[:, u:u + 1], lg[:, u * 8:(u + 1) * 8], AX.X, ALU.max))
        p.drain()
        p.emit("vector", lambda e, u=u: e.tensor_scalar(
            ohf[:, u * 8:(u + 1) * 8], lg[:, u * 8:(u + 1) * 8], lgm[:, u:u + 1], None, ALU.is_ge), inc="v")
        v_oh_map[(si, u)] = p.cnt["v"]

    def emit_gate_fin(si, u):
        bk = 6 if u % 2 == 0 else 2
        p.wait("tensor", "v", v_oh_map[(si, u)])
        bank_wait(bk)
        p.emit("tensor", lambda e, bk=bk, u=u: e.transpose(
            psum[bk][0:8, 0:128], ohf[:, u * 8:(u + 1) * 8], identf[:, :]), inc="t")
        t_oT = p.cnt["t"]
        p.wait("scalar", "t", t_oT)
        og = notT_guard.get(u % 2)
        if og:
            p.wait("scalar", "v", og)
        p.emit("scalar", lambda e, bk=bk, u=u: e.copy(ohT[:, u * 128:(u + 1) * 128], psum[bk][0:8, 0:128]))
        p.emit("scalar", lambda e, bk=bk, u=u: e.copy(
            ohTf[:, (u % 2) * 128:(u % 2 + 1) * 128], psum[bk][0:8, 0:128]), inc="s")
        s_oT = p.cnt["s"]
        bank_guard[bk] = ("s", s_oT)
        p.wait("vector", "s", s_oT)
        p.wait("vector", "dA", nm_write_d[0])
        p.emit("vector", lambda e, u=u: e.tensor_scalar(
            notT[:, u * 128:(u + 1) * 128], ohTf[:, (u % 2) * 128:(u % 2 + 1) * 128],
            -1.0, 1.0, ALU.mult, ALU.add), inc="v")
        notT_guard[u % 2] = p.cnt["v"]
        gate_v.append(p.cnt["v"])

    projT_d = [0]

    def emit_proj_dmats(si):
        p.wait("sync", "t", slab_end_t[0])
        for u in range(NSUB):
            dst = projinT.rearrange("p (c x) -> p c x", c=8)[:, :, u * 128:(u + 1) * 128]
            p.dmat(dst, projin[:, u * 1024:(u + 1) * 1024])
        projT_d[0] = p.cnt["dS"]

    def emit_projsub(si, u):
        s0 = si * SLAB
        dT = projT_d[0]
        xb3 = xs3[:, (u % 2) * 1024:(u % 2 + 1) * 1024]
        guard = xs3_used.get((si, u - 2)) or (xs3_used.get((si - 1, u + 2)) if u < 2 else None)
        if guard:
            p.wait("scalar", "v", guard)
        p.dma("scalar", xb3, x_in[s0 + u * 128:s0 + (u + 1) * 128, :])
        d_x = p.cnt["dA"]
        p.wait("tensor", "dS", dT)
        p.wait("tensor", "s", wp_all)
        for half in range(2):
            bk = 4 + half
            bank_wait(bk)
            for c in range(8):
                lhs = projinT[:, c * 512 + u * 128: c * 512 + (u + 1) * 128]
                rhs = Wpb[:, c * D + half * 512: c * D + (half + 1) * 512]
                p.emit("tensor", lambda e, bk=bk, l=lhs, r=rhs, st=(c == 0), sp=(c == 7):
                       e.matmul(psum[bk][:, :], l, r, start=st, stop=sp),
                       inc="t" if c == 7 else None)
        t_pj = p.cnt["t"]
        p.wait("vector", "t", t_pj)
        p.wait("vector", "dA", d_x)
        p.emit("vector", lambda e, xb3=xb3: e.tensor_tensor(r1[:, 0:512], psum[4][:, :], xb3[:, 0:512], ALU.add))
        p.emit("vector", lambda e, xb3=xb3: e.tensor_tensor(r1[:, 512:1024], psum[5][:, :], xb3[:, 512:1024], ALU.add), inc="v")
        v_r1 = p.cnt["v"]
        bank_guard[4] = ("v", v_r1)
        bank_guard[5] = ("v", v_r1)
        xs3_used[(si, u)] = v_r1
        p.wait("scalar", "v", v_r1)
        layer_norm(r1[:, :], D, t_pj)
        p.emit("vector", lambda e, u=u: e.tensor_scalar(
            xh1[:, u * 1024:(u + 1) * 1024], r1[:, :], s_mu[:, :], s_ri[:, :], ALU.subtract, ALU.mult))
        prev_dmat = xh1T_dS.get((si, u - 2)) or (xh1T_dS.get((si - 1, u + 2)) if u < 2 else None)
        if prev_dmat:
            p.wait("vector", "dS", prev_dmat)
        xhb = xh1b[:, (u % 2) * 1024:(u % 2 + 1) * 1024]
        p.emit("vector", lambda e, xhb=xhb: e.tensor_scalar(
            xhb, r1[:, :], s_mu[:, :], s_ri[:, :], ALU.subtract, ALU.mult), inc="v")
        v_x1 = p.cnt["v"]
        p.wait("sync", "v", v_x1)
        dst = xh1T.rearrange("p (c x) -> p c x", c=8)[:, :, u * 128:(u + 1) * 128]
        p.dmat(dst, xhb)
        xh1T_dS[(si, u)] = p.cnt["dS"]

    emit_q_loads(0)
    for si in range(NS):
        s0 = si * SLAB
        p.wait("scalar", "dS", projT_dS[0])
        exp_done = {0: emit_scores(0)}
        for h in range(H):
            if h + 1 < H:
                exp_done[h + 1] = emit_scores(h + 1)
            emit_av(h, exp_done[h])
        attn_s = p.cnt["s"]
        qs_used_t[0] = p.cnt["t"]
        if si + 1 < NS:
            emit_q_loads(si + 1)
        gate_v.clear()
        p.wait("sync", "s", attn_s)
        emit_proj_dmats(si)
        for u in range(NSUB):
            emit_projsub(si, u)
            if u >= 1:
                emit_gate_mm(si, u - 1)
        emit_gate_mm(si, NSUB - 1)
        for u in range(NSUB):
            emit_gate_fin(si, u)
        p.wait("scalar", "v", gate_v[-1])
        p.dma("scalar", nm_hbm[:, :], notT[:, :])
        nm_write_d[0] = p.cnt["dA"]
        p.wait("scalar", "dA")
        p.wait("scalar", "t", nm_guard_t[0])
        p.dma("scalar", nmflat[0:1, :], bass.AP(nm_hbm, 0, [[1, 1], [1, 8 * SLAB]]))
        d_nm = p.cnt["dA"]
        h_s = {}
        for e_i in range(E):
            p.wait("tensor", "v", ffnw_v)
            p.wait("tensor", "s", ffnw_s)
            p.wait("tensor", "dA", d_nm)
            for hc in range(2):
                bk = hc
                bank_wait(bk)
                for k in range(8):
                    lhs = W1b[:, (e_i * 8 + k) * FH + hc * 128: (e_i * 8 + k) * FH + (hc + 1) * 128]
                    rhs = xh1T[:, k * 512:(k + 1) * 512]
                    p.emit("tensor", lambda e, bk=bk, l=lhs, r=rhs, st=(k == 0):
                           e.matmul(psum[bk][:, :], l, r, start=st, stop=False))
                p.emit("tensor", lambda e, bk=bk, e_i=e_i:
                       e.matmul(psum[bk][:, :], neg30row[:, :],
                                nmflat[0:1, e_i * SLAB:(e_i + 1) * SLAB], start=False, stop=True), inc="t")
                t_h = p.cnt["t"]
                p.wait("scalar", "t", t_h)
                p.emit("scalar", lambda e, bk=bk, e_i=e_i, hc=hc: e.activation(
                    hb[:, e_i * 1024 + hc * 512: e_i * 1024 + (hc + 1) * 512],
                    psum[bk][:, :], AF.Gelu, bias=b1pT[:, e_i * 2 + hc:e_i * 2 + hc + 1]), inc="s")
                bank_guard[bk] = ("s", p.cnt["s"])
            h_s[e_i] = p.cnt["s"]
        nm_guard_t[0] = p.cnt["t"]
        for u in range(NSUB):
            for half in range(2):
                bk = 4 + half
                bank_wait(bk)
                p.wait("tensor", "s", h_s[E - 1])
                p.emit("tensor", lambda e, bk=bk, u=u, half=half:
                       e.matmul(psum[bk][:, :], ohT[:, u * 128:(u + 1) * 128],
                                b2b[:, half * 512:(half + 1) * 512], start=True, stop=False))
                for e_i in range(E):
                    for hc in range(2):
                        lhs = hb[:, e_i * 1024 + hc * 512 + u * 128: e_i * 1024 + hc * 512 + (u + 1) * 128]
                        rhs = W2b[:, (e_i * 2 + hc) * D + half * 512:(e_i * 2 + hc) * D + (half + 1) * 512]
                        last = (e_i == E - 1 and hc == 1)
                        p.emit("tensor", lambda e, bk=bk, l=lhs, r=rhs, sp=last:
                               e.matmul(psum[bk][:, :], l, r, start=False, stop=sp),
                               inc="t" if last else None)
            t_y = p.cnt["t"]
            p.wait("vector", "t", t_y)
            p.emit("vector", lambda e, u=u: e.tensor_tensor(
                r1[:, 0:512], psum[4][:, :], xh1[:, u * 1024:u * 1024 + 512], ALU.add))
            p.emit("vector", lambda e, u=u: e.tensor_tensor(
                r1[:, 512:1024], psum[5][:, :], xh1[:, u * 1024 + 512:(u + 1) * 1024], ALU.add), inc="v")
            v_r2 = p.cnt["v"]
            bank_guard[4] = ("v", v_r2)
            bank_guard[5] = ("v", v_r2)
            p.wait("scalar", "v", v_r2)
            layer_norm(r1[:, :], D, t_y)
            ob = outt[:, :]
            p.wait("vector", "dA", store_d[0])
            p.emit("vector", lambda e, ob=ob: e.tensor_scalar(
                ob, r1[:, :], s_mu[:, :], s_ri[:, :], ALU.subtract, ALU.mult), inc="v")
            v_o = p.cnt["v"]
            p.wait("scalar", "v", v_o)
            p.dma("scalar", y_out[s0 + u * 128:s0 + (u + 1) * 128, :], ob)
            store_d[0] = p.cnt["dA"]
        projT_dS[0] = p.cnt["dS"]
        slab_end_t[0] = p.cnt["t"]

    p.wait("sync", "dA", p.cnt["dA"])
    p.wait("sync", "dS", p.cnt["dS"])

    with nc.Block() as block:
        @block.sync
        def _(eng):
            for f in p.ops["sync"]:
                f(eng)

        @block.tensor
        def _(eng):
            for f in p.ops["tensor"]:
                f(eng)

        @block.vector
        def _(eng):
            for f in p.ops["vector"]:
                f(eng)

        @block.scalar
        def _(eng):
            for f in p.ops["scalar"]:
                f(eng)

        @block.gpsimd
        def _(eng):
            for f in p.ops["gpsimd"]:
                f(eng)

    ctx.close()
    return nc


_NC_CACHE = [None]


def kernel(L0_attn_x, L0_attn_W_attn, L0_attn_W_proj, L0_ffn_Wg, L0_ffn_ln_scale,
           L0_ffn_ln_bias, L0_ffn_W1, L0_ffn_b1, L0_ffn_W2, L0_ffn_b2):
    x = np.ascontiguousarray(np.asarray(L0_attn_x, dtype=np.float32))
    shared = {
        "w_attn": np.ascontiguousarray(np.asarray(L0_attn_W_attn, dtype=np.float32)),
        "w_proj": np.ascontiguousarray(np.asarray(L0_attn_W_proj, dtype=np.float32)),
        "wg_in": np.ascontiguousarray(np.asarray(L0_ffn_Wg, dtype=np.float32)),
        "lns_in": np.ascontiguousarray(np.asarray(L0_ffn_ln_scale, dtype=np.float32)),
        "lnb_in": np.ascontiguousarray(np.asarray(L0_ffn_ln_bias, dtype=np.float32)),
        "w1_in": np.ascontiguousarray(np.asarray(L0_ffn_W1, dtype=np.float32)),
        "b1_in": np.ascontiguousarray(np.asarray(L0_ffn_b1, dtype=np.float32)),
        "w2_in": np.ascontiguousarray(np.asarray(L0_ffn_W2, dtype=np.float32)),
        "b2_in": np.ascontiguousarray(np.asarray(L0_ffn_b2, dtype=np.float32)),
    }
    if _NC_CACHE[0] is None:
        nc = bacc.Bacc("TRN2", target_bir_lowering=False)
        build(nc)
        nc.finalize()
        _NC_CACHE[0] = nc
    nc = _NC_CACHE[0]
    in_maps = [dict(shared, x_in=np.ascontiguousarray(x[b])) for b in range(8)]
    res = run_bass_kernel_spmd(nc, in_maps, core_ids=list(range(8)))
    return np.stack([res.results[b]["y_out"] for b in range(8)], axis=0)


# version 27
# speedup vs baseline: 1.0116x; 1.0116x over previous
import numpy as np
from contextlib import ExitStack

import concourse.bass as bass
import concourse.bacc as bacc
import concourse.mybir as mybir
from concourse.bass_utils import run_bass_kernel_spmd

F32 = mybir.dt.float32
BF16 = mybir.dt.bfloat16
I16 = mybir.dt.int16
U32 = mybir.dt.uint32
ALU = mybir.AluOpType
AF = mybir.ActivationFunctionType
AX = mybir.AxisListType

T, D, H, HDIM, E, FH = 4096, 1024, 16, 64, 8, 256
NS, SLAB, NSUB = 8, 512, 4
KSEL = 512
VOC, VOCP = 50176, 3136
ENGINES = ("sync", "tensor", "vector", "scalar", "gpsimd")

SLOT_MASKED = [False if i < 256 else ((i % 16) * 16 + (i // 16 - 16)) < 121
               for i in range(KSEL)]
assert sum(SLOT_MASKED) == 121


class Prog:
    def __init__(self, nc, ctx):
        self.nc, self.ctx = nc, ctx
        self.ops = {e: [] for e in ENGINES}
        self.sems, self.cnt = {}, {}
        self.lastw = {e: {} for e in ENGINES}

    def sem(self, name):
        self.sems[name] = self.ctx.enter_context(self.nc.semaphore(name))
        self.cnt[name] = 0

    def emit(self, eng, fn, inc=None, amt=1):
        if inc is not None:
            self.cnt[inc] += amt
            s = self.sems[inc]
            self.ops[eng].append(lambda e, fn=fn, s=s, a=amt: fn(e).then_inc(s, a))
            return self.cnt[inc]
        self.ops[eng].append(fn)
        return None

    def wait(self, eng, sem, val=None):
        v = self.cnt[sem] if val is None else val
        if v is None or v <= 0 or self.lastw[eng].get(sem, -1) >= v:
            return
        self.lastw[eng][sem] = v
        s = self.sems[sem]
        self.ops[eng].append(lambda e, s=s, v=v: e.wait_ge(s, v))

    def dma(self, eng, out_ap, in_ap, **kw):
        inc = {"scalar": "dA", "gpsimd": "dG"}.get(eng, "dS")
        return self.emit(eng, lambda e, o=out_ap, i=in_ap, kw=kw: e.dma_start(o, i, **kw),
                         inc=inc, amt=16)

    def dmat(self, out_ap, in_ap):
        return self.emit("sync", lambda e, o=out_ap, i=in_ap: e.dma_start_transpose(o, i),
                         inc="dS", amt=16)

    def drain(self, eng="vector"):
        self.ops[eng].append(lambda e: e.drain())


def build(nc):
    x_in = nc.dram_tensor("x_in", [T, D], F32, kind="ExternalInput")
    w_attn = nc.dram_tensor("w_attn", [D, 3 * D], F32, kind="ExternalInput")
    w_proj = nc.dram_tensor("w_proj", [D, D], F32, kind="ExternalInput")
    wg_in = nc.dram_tensor("wg_in", [E, D], F32, kind="ExternalInput")
    lns_in = nc.dram_tensor("lns_in", [E, D], F32, kind="ExternalInput")
    lnb_in = nc.dram_tensor("lnb_in", [E, D], F32, kind="ExternalInput")
    w1_in = nc.dram_tensor("w1_in", [E, D, FH], F32, kind="ExternalInput")
    b1_in = nc.dram_tensor("b1_in", [E, FH], F32, kind="ExternalInput")
    w2_in = nc.dram_tensor("w2_in", [E, FH, D], F32, kind="ExternalInput")
    b2_in = nc.dram_tensor("b2_in", [E, D], F32, kind="ExternalInput")
    y_out = nc.dram_tensor("y_out", [T, D], F32, kind="ExternalOutput")

    q_hbm = nc.dram_tensor("q_hbm", [8, 128, T], BF16)
    kv_hbm = nc.dram_tensor("kv_hbm", [8, 128, 2 * T], BF16)
    last_hbm = nc.dram_tensor("last_hbm", [16, T], F32)
    t1_hbm = nc.dram_tensor("t1_hbm", [2, 128], F32)
    idx_hbm = nc.dram_tensor("idx_hbm", [2, 128, 32], I16)
    nm_hbm = nc.dram_tensor("nm_hbm", [8, SLAB], BF16)
    lastp_hbm = nc.dram_tensor("lastp_hbm", [8, 16, T], F32)

    ctx = ExitStack()
    p = Prog(nc, ctx)
    for s in ("dA", "dS", "dG", "t", "v", "s", "g"):
        p.sem(s)

    def sb(name, shape, dt=F32):
        return ctx.enter_context(nc.sbuf_tensor(name, shape, dt))

    identf = sb("identf", [128, 128])
    eps_t = sb("eps_t", [128, 1])
    j2scr = sb("j2scr", [128, 1])
    expbias = sb("expbias", [128, 4])
    neg30row = sb("neg30row", [1, 128], BF16)
    ones_col = sb("ones_col", [128, 1], BF16)
    qlast = sb("qlast", [128, 8], BF16)
    blkd = sb("blkd", [128, 128], BF16)
    KT = sb("KT", [128, 8 * KSEL], BF16)
    VO = sb("VO", [128, 64 * 65], BF16)
    psum = [ctx.enter_context(nc.psum_tensor(f"pb{i}", [128, 512], F32)) for i in range(8)]
    bank_guard = {}

    def bank_wait(bk):
        g = bank_guard.get(bk)
        if g:
            p.wait("tensor", g[0], g[1])

    rowi = sb("rowi", [128, 128])
    coli = sb("coli", [128, 128])
    p.emit("gpsimd", lambda e: e.iota(coli[:, :], [[1, 128]], channel_multiplier=0,
                                      allow_small_or_imprecise_dtypes=True), inc="g")
    p.emit("gpsimd", lambda e: e.iota(rowi[:, :], [[0, 128]], channel_multiplier=1,
                                      allow_small_or_imprecise_dtypes=True), inc="g")
    p.wait("vector", "g")
    p.emit("vector", lambda e: e.tensor_tensor(identf[:, :], coli[:, :], rowi[:, :], ALU.is_equal), inc="v")
    v_ident = p.cnt["v"]
    p.emit("vector", lambda e: e.memset(eps_t[:, :], 1e-5))
    p.emit("vector", lambda e: e.memset(neg30row[:, :], -1.0e30))
    p.emit("vector", lambda e: e.memset(ones_col[:, :], 1.0))
    p.emit("vector", lambda e: e.memset(expbias[:, :], 0.0))
    p.emit("vector", lambda e: e.memset(blkd[:, :], 0.0), inc="v")
    p.wait("scalar", "g")
    p.wait("scalar", "v")
    p.dma("scalar", t1_hbm[0, :], rowi[:, 0:1])
    p.wait("scalar", "dA")
    p.dma("scalar", j2scr[:, :], bass.AP(t1_hbm, 0, [[1, 8], [16, 16]]), allow_slow_non_contiguous=True)
    p.wait("vector", "dA")
    p.emit("vector", lambda e: e.tensor_scalar(expbias[:, 2:3], j2scr[:, :], 120.5, -1.0e30, ALU.is_le, ALU.mult))
    p.drain()
    p.emit("vector", lambda e: e.tensor_scalar(expbias[:, 3:4], j2scr[:, :], 112.5, -1.0e30, ALU.is_le, ALU.mult), inc="v")
    p.drain()
    v_const = p.cnt["v"]
    _j2 = [16 * (r % 16) + r // 16 for r in range(128)]
    assert all((_j2[r] <= 120) == SLOT_MASKED[2 * 128 + r] for r in range(128))
    assert all((_j2[r] <= 112) == SLOT_MASKED[3 * 128 + r] for r in range(128))

    ffn_scope_names = None
    stage = sb("stage", [128, 1024])
    Wpb = sb("Wpb", [128, 8 * D], BF16)
    WgT = sb("WgT", [128, 64], BF16)
    W1b = sb("W1b", [128, E * 8 * FH], BF16)
    b1pT = sb("b1pT", [128, 16])
    b2b = sb("b2b", [8, D], BF16)
    seb = sb("seb", [128, 8])
    beb = sb("beb", [128, 8])
    bebb = sb("bebb", [128, 8], BF16)
    b1eb = sb("b1eb", [128, 2])

    p1_scope = ExitStack()

    def sb1(name, shape, dt=F32):
        return p1_scope.enter_context(nc.sbuf_tensor(name, shape, dt))

    xT = sb1("xT", [128, 8 * T], BF16)
    x0_scope = ExitStack()
    xs = x0_scope.enter_context(nc.sbuf_tensor("xs", [128, 2 * 2048], F32))
    xbb = x0_scope.enter_context(nc.sbuf_tensor("xbb", [128, 2 * 2048], BF16))

    slab_order = [7] + list(range(7))
    xT_done = {}
    cast_v = {}
    hparts = [(s, hh) for s in slab_order for hh in range(2)]
    for hi, (s, hh) in enumerate(hparts):
        buf = hi % 2
        if hi >= 2:
            p.wait("scalar", "v", cast_v[hi - 2])
        for u in range(2):
            uu = hh * 2 + u
            p.dma("scalar", xs[:, buf * 2048 + u * 1024: buf * 2048 + (u + 1) * 1024],
                  x_in[s * SLAB + uu * 128: s * SLAB + (uu + 1) * 128, :])
        dl = p.cnt["dA"]
        p.wait("vector", "dA", dl)
        if hi >= 2:
            p.wait("vector", "dS", xT_done[hi - 2])
        p.emit("vector", lambda e, b=buf: e.tensor_copy(
            xbb[:, b * 2048:(b + 1) * 2048], xs[:, b * 2048:(b + 1) * 2048]), inc="v")
        cast_v[hi] = p.cnt["v"]
        p.wait("sync", "v", cast_v[hi])
        for u in range(2):
            uu = hh * 2 + u
            dst = xT.rearrange("p (c x) -> p c x", c=8)[:, :, s * SLAB + uu * 128: s * SLAB + (uu + 1) * 128]
            srcb = xbb[:, buf * 2048 + u * 1024: buf * 2048 + (u + 1) * 1024]
            p.dmat(dst, srcb)
        xT_done[hi] = p.cnt["dS"]
    xT_all = p.cnt["dS"]
    x0_scope.close()

    Wst = sb1("Wst", [128, 1024])
    Wmb = sb1("Wmb", [128, 2 * 1024], BF16)
    qst = sb1("qst", [128, 2 * 512], BF16)

    wm_cast_by_buf = [0, 0]
    wm_used_t = [0, 0]

    def load_wm(m, buf):
        p.wait("scalar", "s", wm_cast_by_buf[0])
        p.dma("scalar", Wst[:, :],
              bass.AP(w_attn, m * 128, [[3072, 128], [128 * 3072, 8], [1, 128]]))
        d = p.cnt["dA"]
        p.wait("scalar", "dA", d)
        p.wait("scalar", "t", wm_used_t[buf])
        p.emit("scalar", lambda e, b=buf: e.copy(
            Wmb[:, b * 1024:(b + 1) * 1024], Wst[:, :]), inc="s")
        wm_cast_by_buf[0] = p.cnt["s"]
        return p.cnt["s"]

    p.wait("tensor", "dS", xT_done[1])
    qlast_wait = {0: load_wm(0, 0)}
    for m in range(8):
        buf = m % 2
        if m + 1 < 8:
            qlast_wait[m + 1] = load_wm(m + 1, (m + 1) % 2)
        p.wait("tensor", "s", qlast_wait[m])
        for k in range(8):
            lhs = Wmb[:, buf * 1024 + k * 128:(buf * 1024) + (k + 1) * 128]
            rhs = xT[:, k * T + 4095: k * T + 4096]
            p.emit("tensor", lambda e, l=lhs, r=rhs, m=m, st=(k == 0), sp=(k == 7):
                   e.matmul(psum[5][0:128, m:m + 1], l, r, start=st, stop=sp),
                   inc="t" if k == 7 else None)
        wm_used_t[buf] = p.cnt["t"]
    t_qlast = p.cnt["t"]
    p.wait("scalar", "t", t_qlast)
    p.emit("scalar", lambda e: e.copy(qlast[:, :], psum[5][0:128, 0:8]), inc="s")
    s_qlast = p.cnt["s"]
    bank_guard[5] = ("s", s_qlast)
    p.wait("vector", "s", s_qlast)
    for c in range(8):
        base = c * 16
        p.emit("vector", lambda e, c=c, base=base: e.tensor_copy(
            blkd[0:64, base + 2 * c:base + 2 * c + 1], qlast[0:64, c:c + 1]))
        p.emit("vector", lambda e, c=c, base=base: e.tensor_copy(
            blkd[64:128, base + 2 * c + 1:base + 2 * c + 2], qlast[64:128, c:c + 1]))
    p.emit("vector", lambda e: e.nop(), inc="v")
    blkd_v = p.cnt["v"]

    m_order = []
    for c in range(8):
        m_order += [8 + c, 16 + c]
    m_order += list(range(8))

    wp_state = {"prev_s": 0, "last_prep": 0}

    def wload(dst_ap, src_ap, wait_s=None, **kw):
        if wait_s:
            p.wait("sync", "s", wait_s)
        p.dma("sync", dst_ap, src_ap, **kw)
        return p.cnt["dS"]

    def wcast(dst_ap, src_ap, d):
        p.wait("scalar", "dS", d)
        p.emit("scalar", lambda e, o=dst_ap, i=src_ap: e.copy(o, i), inc="s")
        return p.cnt["s"]

    def p2b_wproj():
        for k in range(8):
            d = wload(stage[:, 0:1024], w_proj[k * 128:(k + 1) * 128, :], wait_s=wp_state["prev_s"])
            wp_state["prev_s"] = wcast(Wpb[:, k * D:(k + 1) * D], stage[:, 0:1024], d)
        wp_state["wp_all"] = wp_state["prev_s"]

    def p2b_misc():
        d = 0
        for k in range(8):
            d = wload(stage[:, 8 * k:8 * (k + 1)],
                      bass.AP(wg_in, k * 128, [[1, 128], [D, 8]]),
                      wait_s=(wp_state["prev_s"] if k == 0 else None),
                      allow_slow_non_contiguous=True)
        wp_state["wgt_s"] = wcast(WgT[:, :], stage[:, 0:64], d)
        d = wload(stage[0:8, 0:1024], b2_in[:, :], wait_s=wp_state["wgt_s"])
        wp_state["prev_s"] = wcast(b2b[:, :], stage[0:8, 0:1024], d)

    def p2b_expert(e_i):
        wload(seb[:, :], bass.AP(lns_in, e_i * D, [[1, 128], [128, 8]]),
              wait_s=wp_state["prev_s"], allow_slow_non_contiguous=True)
        wload(beb[:, :], bass.AP(lnb_in, e_i * D, [[1, 128], [128, 8]]),
              allow_slow_non_contiguous=True)
        wload(b1eb[:, :], bass.AP(b1_in, e_i * FH, [[1, 128], [128, 2]]),
              allow_slow_non_contiguous=True)
        d_se = p.cnt["dS"]
        for rr in range(2):
            for kk in range(4):
                k = rr * 4 + kk
                wload(stage[:, kk * FH:(kk + 1) * FH], w1_in[e_i, k * 128:(k + 1) * 128, :])
            d_w1 = p.cnt["dS"]
            p.wait("scalar", "dS", d_w1)
            p.emit("scalar", lambda e, e_i=e_i, rr=rr: e.copy(
                W1b[:, (e_i * 8 + rr * 4) * FH:(e_i * 8 + rr * 4 + 4) * FH],
                stage[:, 0:4 * FH]), inc="s")
            if rr == 0:
                p.wait("sync", "s", p.cnt["s"])
        p.wait("scalar", "dS", d_se)
        p.emit("scalar", lambda e: e.copy(bebb[:, :], beb[:, :]), inc="s")
        raw_s = p.cnt["s"]
        wp_state["prev_s"] = raw_s
        p.wait("tensor", "s", raw_s)
        for hc in range(2):
            bank_wait(5)
            for k in range(8):
                lhs = W1b[:, e_i * 8 * FH + k * FH + hc * 128: e_i * 8 * FH + k * FH + (hc + 1) * 128]
                p.emit("tensor",
                       lambda e, l=lhs, r=bebb[:, 0:8], k=k, st=(k == 0), sp=(k == 7):
                       e.matmul(psum[5][0:128, 0:1], l, r[:, k:k + 1], start=st, stop=sp),
                       inc="t" if k == 7 else None)
            t_b1 = p.cnt["t"]
            p.wait("vector", "t", t_b1)
            p.wait("vector", "dS", d_se)
            p.emit("vector", lambda e, e_i=e_i, hc=hc: e.tensor_tensor(
                b1pT[:, e_i * 2 + hc: e_i * 2 + hc + 1], psum[5][0:128, 0:1],
                b1eb[:, hc:hc + 1], ALU.add), inc="v")
            bank_guard[5] = ("v", p.cnt["v"])
        for k in range(8):
            p.emit("vector", lambda e, e_i=e_i, k=k: e.tensor_scalar(
                W1b[:, (e_i * 8 + k) * FH:(e_i * 8 + k + 1) * FH],
                W1b[:, (e_i * 8 + k) * FH:(e_i * 8 + k + 1) * FH],
                seb[:, k:k + 1], None, ALU.mult))
        p.emit("vector", lambda e: e.nop(), inc="v")
        wp_state["fold_v"] = p.cnt["v"]

    p2b_units = [p2b_wproj, p2b_misc] + [lambda e_i=e_i: p2b_expert(e_i) for e_i in range(E)]

    p2_scope = ExitStack()

    def sb2(name, shape, dt=F32):
        return p2_scope.enter_context(nc.sbuf_tensor(name, shape, dt))

    tk1 = [sb2(f"tk1_{g}", [128, 32], U32) for g in range(2)]
    tk2 = [sb2(f"tk2_{g}", [128, 32], U32) for g in range(2)]
    t1t = sb2("t1t", [128, 1])
    idx16 = sb2("idx16", [128, 32], I16)
    idxrep = sb2("idxrep", [128, 32], I16)
    kvG = sb2("kvG", [128, 2 * KSEL], BF16)
    VFt8 = sb2("VFt8", [128, 8 * KSEL], BF16)
    identb = sb2("identb", [128, 128], BF16)
    big30 = sb2("big30", [1, 64])
    scA = sb2("scA", [128, VOCP])
    sc2 = sb2("sc2", [128, VOCP])
    kvst_scope = ExitStack()
    kvstage = kvst_scope.enter_context(nc.sbuf_tensor("kvstage", [128, 8 * 1024], BF16))
    lastev = kvst_scope.enter_context(nc.sbuf_tensor("lastev", [16, 512], F32))
    kvL_holder = []

    sel = {"lastout": {}, "idx_ready": {}, "primed": False}

    def emit_sel_group(g):
        if not sel["primed"]:
            sel["primed"] = True
            p.wait("vector", "v", v_ident)
            p.emit("vector", lambda e: e.tensor_copy(identb[:, :], identf[:, :]))
            p.emit("vector", lambda e: e.memset(big30[:, :], 1.0e30), inc="v")
            sel["identb_v"] = p.cnt["v"]
            p.wait("sync", "v")
            p.dma("sync", bass.AP(t1_hbm, 128, [[1, 64]]), big30[0:1, :])
            sel["d_b30"] = p.cnt["dS"]
        if g == 0:
            p.emit("vector", lambda e: e.memset(scA[:, :], -1.0e30), inc="v")
            p.wait("sync", "v")
        else:
            p.wait("sync", "v", sel["ko_v"])
            p.wait("sync", "g", sel["tkA0_g"])
        p.wait("sync", "dA", sel["lastout"][g])
        p.dma("sync", scA[0:128:16, 0:VOCP],
              bass.AP(last_hbm, g * 8 * T, [[T, 8], [1, VOCP]]))
        p.dma("sync", scA[1:128:16, 0:T - VOCP],
              bass.AP(last_hbm, g * 8 * T + VOCP, [[T, 8], [1, T - VOCP]]))
        p.wait("sync", "dS", sel["d_b30"])
        p.dma("sync", scA[1:128:16, 896:960], bass.AP(t1_hbm, 128, [[0, 8], [1, 64]]))
        p.wait("gpsimd", "dS")
        p.emit("gpsimd", lambda e, g=g: e.topk(tk1[g][:, :], scA[:, :], tokens=8, vocab_size=VOC, k=256), inc="g")
        if g == 0:
            sel["tkA0_g"] = p.cnt["g"]
        p.wait("sync", "g")
        p.dma("sync", t1_hbm[g, :], tk1[g][:, 0:1].bitcast(F32))
        p.wait("sync", "dS")
        p.dma("sync", t1t[:, :], bass.AP(t1_hbm, g * 128, [[16, 8], [0, 16], [1, 1]]))

    def emit_sel_group_p2(g):
        p.wait("vector", "dS")
        p.emit("vector", lambda e: e.tensor_scalar(sc2[:, :], scA[:, :], t1t[:, :], -1.0e30, ALU.is_ge, ALU.mult))
        p.drain()
        p.emit("vector", lambda e: e.tensor_tensor(sc2[:, :], scA[:, :], sc2[:, :], ALU.add), inc="v")
        if g == 0:
            sel["ko_v"] = p.cnt["v"]
        p.wait("gpsimd", "v")
        p.emit("gpsimd", lambda e, g=g: e.topk(tk2[g][:, :], sc2[:, :], tokens=8, vocab_size=VOC, k=256), inc="g")

    def emit_sel_group_p3(g):
        p.wait("vector", "g")
        p.emit("vector", lambda e, g=g: e.tensor_copy(idx16[:, 0:16], tk1[g][:, 16:32]))
        p.emit("vector", lambda e, g=g: e.tensor_copy(idx16[:, 16:32], tk2[g][:, 16:32]), inc="v")
        p.wait("sync", "v")
        p.dma("sync", idx_hbm[g, :, :], idx16[:, :])
        sel["idx_ready"][g] = p.cnt["dS"]

    gat = {"kvload": {}, "gath_g": {}, "unp_v": {}}

    def emit_gathers(pairs):
        if not kvL_holder:
            kvst_scope.close()
            kvL_holder.append(p2_scope.enter_context(nc.sbuf_tensor("kvL", [128, 8192], BF16)))
        kvL = kvL_holder[0]

        def kv_load(c):
            p.dma("gpsimd", kvL[:, :], kv_hbm[c, :, :])
            gat["kvload"][c] = p.cnt["dG"]

        if pairs[0] == 0:
            p.wait("gpsimd", "dA", sel["lastout"][1])
            kv_load(0)
        for c in pairs:
            g = c // 4
            p.wait("sync", "dS", sel["idx_ready"][g])
            if c >= 1:
                p.wait("sync", "g", gat["gath_g"][c - 1])
            for half in range(2):
                p.dma("sync", idxrep[half * 64:(half + 1) * 64, :],
                      bass.AP(idx_hbm, g * 128 * 32 + ((2 * c + half) % 8) * 16 * 32,
                              [[0, 4], [32, 16], [1, 32]]))
            d_rep = p.cnt["dS"]
            p.wait("gpsimd", "dS", d_rep)
            p.wait("gpsimd", "dG", gat["kvload"][c])
            if c >= 1:
                p.wait("gpsimd", "v", gat["unp_v"][c - 1])
            p.emit("gpsimd", lambda e, c=c: e.ap_gather(
                kvG[:, :], kvL[:, :], idxrep[:, :],
                channels=128, num_elems=T, d=2, num_idxs=KSEL), inc="g")
            gat["gath_g"][c] = p.cnt["g"]
            if c + 1 < 8:
                kv_load(c + 1)
            p.wait("vector", "g", gat["gath_g"][c])
            kvG2 = kvG.rearrange("p (x two) -> p x two", two=2)
            p.emit("vector", lambda e, c=c, kvG2=kvG2: e.tensor_copy(KT[:, c * KSEL:(c + 1) * KSEL], kvG2[:, :, 0:1]))
            p.emit("vector", lambda e, c=c, kvG2=kvG2: e.tensor_copy(
                VFt8[:, c * KSEL:(c + 1) * KSEL], kvG2[:, :, 1:2]), inc="v")
            gat["unp_v"][c] = p.cnt["v"]
        sel["unp_v"] = gat["unp_v"]

    wm_cast = {}
    wm_cast[m_order[0]] = load_wm(m_order[0], 0)
    kv_out_d = {c: 0 for c in range(8)}
    qst_dma = [0, 0]
    kvst_dma = {}
    last_started = [False] * 8
    for mi, m in enumerate(m_order):
        buf = mi % 2
        if mi + 1 < 24:
            nm_ = m_order[mi + 1]
            wm_cast[nm_] = load_wm(nm_, (mi + 1) % 2)
        p.wait("tensor", "s", wm_cast[m])
        p.wait("tensor", "dS", xT_all)
        for hp in range(2):
            for k in range(8):
                lhs = Wmb[:, buf * 1024 + k * 128: buf * 1024 + (k + 1) * 128]
                for sl in range(4):
                    s4 = hp * 4 + sl
                    if k == 0:
                        g = bank_guard.get(sl)
                        if g:
                            p.wait("tensor", g[0], g[1])
                    rhs = xT[:, k * T + s4 * SLAB: k * T + (s4 + 1) * SLAB]
                    p.emit("tensor", lambda e, l=lhs, r=rhs, sl=sl, st=(k == 0), sp=(k == 7):
                           e.matmul(psum[sl][:, :], l, r, start=st, stop=sp),
                           inc="t" if (k == 7 and sl == 3) else None)
            t_mm = p.cnt["t"]
            for sl in range(4):
                s4 = hp * 4 + sl
                if m < 8:
                    qb = qst[:, (s4 % 2) * 512:(s4 % 2 + 1) * 512]
                    p.wait("scalar", "t", t_mm)
                    p.wait("scalar", "dA", qst_dma[s4 % 2])
                    ev = p.emit("scalar", lambda e, o=qb, i=psum[sl][:, :]: e.copy(o, i), inc="s")
                    p.wait("scalar", "s", ev)
                    p.dma("scalar", q_hbm[m, :, s4 * SLAB:(s4 + 1) * SLAB], qb)
                    qst_dma[s4 % 2] = p.cnt["dA"]
                    bank_guard[sl] = ("s", ev)
                else:
                    c = m - 8 if m < 16 else m - 16
                    which = 0 if m < 16 else 1
                    dst = kvstage[:, s4 * 1024:(s4 + 1) * 1024].rearrange(
                        "p (x two) -> p x two", two=2)[:, :, which:which + 1]
                    p.wait("vector", "t", t_mm)
                    p.wait("vector", "dA", kvst_dma.get(s4, 0))
                    p.emit("vector", lambda e, o=dst, i=psum[sl][:, :]: e.tensor_copy(o, i), inc="v")
                    bank_guard[sl] = ("v", p.cnt["v"])
        wm_used_t[buf] = p.cnt["t"]
        v_kv = p.cnt["v"]
        if 16 <= m < 24:
            c = m - 16
            p.wait("tensor", "v", blkd_v)
            p.wait("tensor", "v", v_kv)
            for s4 in range(8):
                lb = (4, 6, 7)[s4 // 3]
                pb = 32 * (s4 % 3)
                rhs = kvstage[:, s4 * 1024:(s4 + 1) * 1024].rearrange(
                    "p (x two) -> p x two", two=2)[:, :, 0:1]
                st0 = not last_started[s4]
                p.emit("tensor", lambda e, l=blkd[:, c * 16:(c + 1) * 16], r=rhs, lb=lb, pb=pb,
                       st=st0, sp=(c == 7):
                       e.matmul(psum[lb][pb:pb + 16, 0:512], l, r, start=st, stop=sp),
                       inc="t" if s4 == 7 else None)
                last_started[s4] = True
            t_last = p.cnt["t"]
            p.wait("scalar", "v", v_kv)
            for s4 in range(8):
                p.dma("scalar", kv_hbm[c, :, s4 * 2 * SLAB:(s4 + 1) * 2 * SLAB],
                      kvstage[:, s4 * 1024:(s4 + 1) * 1024])
                kvst_dma[s4] = p.cnt["dA"]
            kv_out_d[c] = p.cnt["dA"]
            if c in (3, 7):
                gsel = 0 if c == 3 else 1
                ro = gsel * 8
                p.wait("scalar", "t", t_last)
                for s4 in range(8):
                    lb = (4, 6, 7)[s4 // 3]
                    pb = 32 * (s4 % 3)
                    if s4 > 0:
                        p.wait("scalar", "dA", p.cnt["dA"])
                    p.emit("scalar", lambda e, lb=lb, pb=pb: e.copy(
                        lastev[0:16, 0:512], psum[lb][pb:pb + 16, 0:512]), inc="s")
                    p.wait("scalar", "s", p.cnt["s"])
                    p.dma("scalar", bass.AP(last_hbm, ro * T + s4 * 512, [[T, 8], [1, 512]]),
                          lastev[ro:ro + 8, 0:512])
                sel["lastout"][gsel] = p.cnt["dA"]
                if c == 7:
                    for bb in (4, 6, 7):
                        bank_guard[bb] = ("s", p.cnt["s"])
        if 1 <= mi <= 10:
            p2b_units[mi - 1]()
        if mi == 7:
            emit_sel_group(0)
        if mi == 10:
            emit_sel_group_p2(0)
        if mi == 13:
            emit_sel_group_p3(0)
        if mi == 15:
            emit_gathers(range(0, 4))
            emit_sel_group(1)
            emit_sel_group_p2(1)
            emit_sel_group_p3(1)
            emit_gathers(range(4, 8))
    p1_t = p.cnt["t"]
    kv_all_out2 = kv_out_d[7]
    wp_all = wp_state["wp_all"]
    wgt_v = wp_state["wgt_s"]
    ffnw_v = wp_state["fold_v"]
    ffnw_s = wp_state["prev_s"]

    qstore_d = p.cnt["dA"]

    unp_v = sel["unp_v"]
    for c in range(8):
        p.wait("tensor", "v", unp_v[c])
        p.wait("tensor", "v", sel["identb_v"])
        for hh in range(2):
            h = 2 * c + hh
            for kc in range(4):
                bk = 3 + (kc % 2) * 4
                bank_wait(bk)
                p.emit("tensor", lambda e, bk=bk, hh=hh, c=c,
                       i=VFt8[hh * 64:(hh + 1) * 64, c * KSEL + kc * 128:c * KSEL + (kc + 1) * 128]:
                       e.transpose(psum[bk].bitcast(BF16)[0:128, 0:64], i,
                                   identb[hh * 64:(hh + 1) * 64, hh * 64:(hh + 1) * 64]), inc="t")
                t_tr = p.cnt["t"]
                vo = VO[:, (h * 4 + kc) * 65:(h * 4 + kc) * 65 + 65]
                p.wait("scalar", "t", t_tr)
                p.emit("scalar", lambda e, o=vo, bk=bk: e.copy(o[:, 0:64], psum[bk].bitcast(BF16)[0:128, 0:64]), inc="s")
                p.emit("scalar", lambda e, o=vo: e.copy(o[:, 64:65], ones_col[:, :]), inc="s")
                bank_guard[bk] = ("s", p.cnt["s"])
    vo_s = p.cnt["s"]
    p2_scope.close()
    p1_scope.close()

    qs = sb("qs", [128, 4096], BF16)
    p.wait("sync", "dA", qstore_d)
    for c in range(8):
        p.dma("sync", qs[:, c * 512:(c + 1) * 512], q_hbm[c, :, 0:SLAB])
    q0_d = p.cnt["dS"]

    W2b = sb("W2b", [128, E * 2 * D], BF16)
    p.wait("sync", "g")
    w2_prev = [ffnw_s]
    for e_i in range(E):
        for hc in range(2):
            d = wload(stage[:, 0:1024], w2_in[e_i, hc * 128:(hc + 1) * 128, :], wait_s=w2_prev[0])
            w2_prev[0] = wcast(W2b[:, (e_i * 2 + hc) * D:(e_i * 2 + hc + 1) * D], stage[:, 0:1024], d)
    ffnw_s = w2_prev[0]

    xs3 = sb("xs3", [128, 2 * 1024])
    Eh = sb("Eh", [128, 2 * 2048], BF16)
    rc_t = sb("rc_t", [128, 1])
    projin = sb("projin", [128, 4 * 1024], BF16)
    projinT = sb("projinT", [128, 4096], BF16)
    xh1 = sb("xh1", [128, 4 * 1024])
    xh1b = sb("xh1b", [128, 2048], BF16)
    xh1T = sb("xh1T", [128, 4096], BF16)
    hb = sb("hb", [128, E * 1024], BF16)
    r1 = sb("r1", [128, 1024])
    outt = sb("outt", [128, 1024])
    lg = sb("lg", [128, 32])
    lgm = sb("lgm", [128, 4])
    ohf = sb("ohf", [128, 32])
    ohT = sb("ohT", [8, SLAB], BF16)
    ohTf = sb("ohTf", [8, 256], BF16)
    notT = sb("notT", [8, SLAB], BF16)
    nmflat = sb("nmflat", [1, 8 * SLAB], BF16)
    s_mu = sb("s_mu", [128, 1])
    s_sq = sb("s_sq", [128, 1])
    s_m2 = sb("s_m2", [128, 1])
    s_var = sb("s_var", [128, 1])
    s_sd = sb("s_sd", [128, 1])
    s_ri = sb("s_ri", [128, 1])

    def layer_norm(src_ap, n, t_guard):
        p.drain()
        p.emit("vector", lambda e: e.tensor_reduce(s_mu[:, :], src_ap, AX.X, ALU.add), inc="v")
        p.emit("scalar", lambda e, n=n: e.activation(Eh[:, 0:n], src_ap, AF.Square, accum_out=s_sq[:, :]), inc="s")
        s_c = p.cnt["s"]
        p.wait("vector", "s", s_c)
        p.drain()
        p.emit("vector", lambda e, n=n: e.tensor_scalar(s_mu[:, :], s_mu[:, :], 1.0 / n, None, ALU.mult))
        p.drain()
        p.emit("vector", lambda e, n=n: e.tensor_scalar(s_sq[:, :], s_sq[:, :], 1.0 / n, None, ALU.mult))
        p.drain()
        p.emit("vector", lambda e: e.tensor_tensor(s_m2[:, :], s_mu[:, :], s_mu[:, :], ALU.mult))
        p.drain()
        p.emit("vector", lambda e: e.tensor_tensor(s_var[:, :], s_sq[:, :], s_m2[:, :], ALU.subtract), inc="v")
        v_c = p.cnt["v"]
        p.wait("scalar", "v", v_c)
        p.emit("scalar", lambda e: e.activation(s_sd[:, :], s_var[:, :], AF.Sqrt, bias=eps_t[:, :]), inc="s")
        s_c2 = p.cnt["s"]
        p.wait("vector", "s", s_c2)
        p.emit("vector", lambda e: e.reciprocal(s_ri[:, :], s_sd[:, :]), inc="v")
        p.drain()
        return p.cnt["v"]

    store_d = [0, 0]
    qs_used_t = [0]
    q_d_cur = [0]
    xs3_used = {}
    eh_guard = [0, 0]
    av_s_prev = [0]
    projT_dS = [0]
    slab_end_t = [0]
    nm_guard_t = [0]
    nm_write_d = [0]
    gate_v = []
    xh1T_dS = {}

    def emit_q_loads(si):
        s0 = si * SLAB
        p.wait("scalar", "t", qs_used_t[0])
        for c in range(8):
            p.dma("scalar", qs[:, c * 512:(c + 1) * 512], q_hbm[c, :, s0:s0 + SLAB])
        q_d_cur[0] = ("dA", p.cnt["dA"])

    def emit_scores(h):
        pair, off = h // 2, 64 * (h % 2)
        ebuf = h % 2
        p.wait("tensor", q_d_cur[0][0], q_d_cur[0][1])
        p.wait("tensor", "s", vo_s)
        for kc in range(4):
            bk = kc % 2
            bank_wait(bk)
            lhs = KT[off:off + 64, pair * KSEL + kc * 128: pair * KSEL + (kc + 1) * 128]
            rhs = qs[off:off + 64, pair * 512:(pair + 1) * 512]
            p.emit("tensor", lambda e, bk=bk, l=lhs, r=rhs:
                   e.matmul(psum[bk][:, :], l, r, start=True, stop=True), inc="t")
            t_sc = p.cnt["t"]
            p.wait("scalar", "t", t_sc)
            p.wait("scalar", "t", eh_guard[ebuf])
            p.emit("scalar", lambda e, bk=bk, ebuf=ebuf, kc=kc: e.activation(
                Eh[:, ebuf * 2048 + kc * 512: ebuf * 2048 + (kc + 1) * 512],
                psum[bk][:, :], AF.Exp, bias=expbias[:, kc:kc + 1], scale=0.125), inc="s")
            bank_guard[bk] = ("s", p.cnt["s"])
        return p.cnt["s"]

    def emit_av(h, exp_done):
        ebuf = h % 2
        for u in range(NSUB):
            bk = 2 + (u % 2)
            bank_wait(bk)
            p.wait("tensor", "s", exp_done)
            for kc in range(4):
                lhs = Eh[:, ebuf * 2048 + kc * 512 + u * 128: ebuf * 2048 + kc * 512 + (u + 1) * 128]
                rhs = VO[:, (h * 4 + kc) * 65:(h * 4 + kc) * 65 + 65]
                p.emit("tensor", lambda e, bk=bk, l=lhs, r=rhs, st=(kc == 0), sp=(kc == 3):
                       e.matmul(psum[bk][0:128, 0:65], l, r, start=st, stop=sp),
                       inc="t" if kc == 3 else None)
            t_av = p.cnt["t"]
            if u == 3:
                eh_guard[ebuf] = t_av
            p.wait("vector", "t", t_av)
            p.wait("vector", "s", av_s_prev[0])
            p.emit("vector", lambda e, bk=bk: e.reciprocal(rc_t[:, :], psum[bk][0:128, 64:65]), inc="v")
            v_rc = p.cnt["v"]
            p.wait("scalar", "v", v_rc)
            p.emit("scalar", lambda e, bk=bk, u=u, h=h: e.activation(
                projin[:, u * 1024 + h * 64: u * 1024 + (h + 1) * 64],
                psum[bk][0:128, 0:64], AF.Copy, scale=rc_t[:, :]), inc="s")
            av_s_prev[0] = p.cnt["s"]
            bank_guard[bk] = ("s", p.cnt["s"])

    v_oh_map = {}
    notT_guard = {}

    def emit_gate_mm(si, u):
        p.wait("tensor", "dS", xh1T_dS[(si, u)])
        p.wait("tensor", "s", wgt_v)
        bank_wait(6)
        for c in range(8):
            lhs = xh1T[:, c * 512 + u * 128: c * 512 + (u + 1) * 128]
            p.emit("tensor", lambda e, l=lhs, c=c, st=(c == 0), sp=(c == 7):
                   e.matmul(psum[6][0:128, 0:8], l, WgT[:, c * 8:(c + 1) * 8], start=st, stop=sp),
                   inc="t" if c == 7 else None)
        t_g = p.cnt["t"]
        p.wait("scalar", "t", t_g)
        p.emit("scalar", lambda e, u=u: e.copy(lg[:, u * 8:(u + 1) * 8], psum[6][0:128, 0:8]), inc="s")
        s_lg = p.cnt["s"]
        bank_guard[6] = ("s", s_lg)
        p.wait("vector", "s", s_lg)
        p.emit("vector", lambda e, u=u: e.tensor_reduce(
            lgm[:, u:u + 1], lg[:, u * 8:(u + 1) * 8], AX.X, ALU.max))
        p.drain()
        p.emit("vector", lambda e, u=u: e.tensor_scalar(
            ohf[:, u * 8:(u + 1) * 8], lg[:, u * 8:(u + 1) * 8], lgm[:, u:u + 1], None, ALU.is_ge), inc="v")
        v_oh_map[(si, u)] = p.cnt["v"]

    def emit_gate_fin(si, u):
        bk = 6 if u % 2 == 0 else 2
        p.wait("tensor", "v", v_oh_map[(si, u)])
        bank_wait(bk)
        p.emit("tensor", lambda e, bk=bk, u=u: e.transpose(
            psum[bk][0:8, 0:128], ohf[:, u * 8:(u + 1) * 8], identf[:, :]), inc="t")
        t_oT = p.cnt["t"]
        p.wait("scalar", "t", t_oT)
        og = notT_guard.get(u % 2)
        if og:
            p.wait("scalar", "v", og)
        p.emit("scalar", lambda e, bk=bk, u=u: e.copy(ohT[:, u * 128:(u + 1) * 128], psum[bk][0:8, 0:128]))
        p.emit("scalar", lambda e, bk=bk, u=u: e.copy(
            ohTf[:, (u % 2) * 128:(u % 2 + 1) * 128], psum[bk][0:8, 0:128]), inc="s")
        s_oT = p.cnt["s"]
        bank_guard[bk] = ("s", s_oT)
        p.wait("vector", "s", s_oT)
        p.wait("vector", "dA", nm_write_d[0])
        p.emit("vector", lambda e, u=u: e.tensor_scalar(
            notT[:, u * 128:(u + 1) * 128], ohTf[:, (u % 2) * 128:(u % 2 + 1) * 128],
            -1.0, 1.0, ALU.mult, ALU.add), inc="v")
        notT_guard[u % 2] = p.cnt["v"]
        gate_v.append(p.cnt["v"])

    projT_d = [0]

    def emit_proj_dmats(si):
        p.wait("sync", "t", slab_end_t[0])
        for u in range(NSUB):
            dst = projinT.rearrange("p (c x) -> p c x", c=8)[:, :, u * 128:(u + 1) * 128]
            p.dmat(dst, projin[:, u * 1024:(u + 1) * 1024])
        projT_d[0] = p.cnt["dS"]

    def emit_projsub(si, u):
        s0 = si * SLAB
        dT = projT_d[0]
        xb3 = xs3[:, (u % 2) * 1024:(u % 2 + 1) * 1024]
        guard = xs3_used.get((si, u - 2)) or (xs3_used.get((si - 1, u + 2)) if u < 2 else None)
        if guard:
            p.wait("scalar", "v", guard)
        p.dma("scalar", xb3, x_in[s0 + u * 128:s0 + (u + 1) * 128, :])
        d_x = p.cnt["dA"]
        p.wait("tensor", "dS", dT)
        p.wait("tensor", "s", wp_all)
        for half in range(2):
            bk = 4 + half
            bank_wait(bk)
            for c in range(8):
                lhs = projinT[:, c * 512 + u * 128: c * 512 + (u + 1) * 128]
                rhs = Wpb[:, c * D + half * 512: c * D + (half + 1) * 512]
                p.emit("tensor", lambda e, bk=bk, l=lhs, r=rhs, st=(c == 0), sp=(c == 7):
                       e.matmul(psum[bk][:, :], l, r, start=st, stop=sp),
                       inc="t" if c == 7 else None)
        t_pj = p.cnt["t"]
        p.wait("vector", "t", t_pj)
        p.wait("vector", "dA", d_x)
        p.emit("vector", lambda e, xb3=xb3: e.tensor_tensor(r1[:, 0:512], psum[4][:, :], xb3[:, 0:512], ALU.add))
        p.emit("vector", lambda e, xb3=xb3: e.tensor_tensor(r1[:, 512:1024], psum[5][:, :], xb3[:, 512:1024], ALU.add), inc="v")
        v_r1 = p.cnt["v"]
        bank_guard[4] = ("v", v_r1)
        bank_guard[5] = ("v", v_r1)
        xs3_used[(si, u)] = v_r1
        p.wait("scalar", "v", v_r1)
        layer_norm(r1[:, :], D, t_pj)
        p.emit("vector", lambda e, u=u: e.tensor_scalar(
            xh1[:, u * 1024:(u + 1) * 1024], r1[:, :], s_mu[:, :], s_ri[:, :], ALU.subtract, ALU.mult))
        prev_dmat = xh1T_dS.get((si, u - 2)) or (xh1T_dS.get((si - 1, u + 2)) if u < 2 else None)
        if prev_dmat:
            p.wait("vector", "dS", prev_dmat)
        xhb = xh1b[:, (u % 2) * 1024:(u % 2 + 1) * 1024]
        p.emit("vector", lambda e, xhb=xhb: e.tensor_scalar(
            xhb, r1[:, :], s_mu[:, :], s_ri[:, :], ALU.subtract, ALU.mult), inc="v")
        v_x1 = p.cnt["v"]
        p.wait("sync", "v", v_x1)
        dst = xh1T.rearrange("p (c x) -> p c x", c=8)[:, :, u * 128:(u + 1) * 128]
        p.dmat(dst, xhb)
        xh1T_dS[(si, u)] = p.cnt["dS"]

    q_d_cur[0] = ("dS", q0_d)
    for si in range(NS):
        s0 = si * SLAB
        p.wait("scalar", "dS", projT_dS[0])
        exp_done = {0: emit_scores(0)}
        for h in range(H):
            if h + 1 < H:
                exp_done[h + 1] = emit_scores(h + 1)
            emit_av(h, exp_done[h])
        attn_s = p.cnt["s"]
        qs_used_t[0] = p.cnt["t"]
        if si + 1 < NS:
            emit_q_loads(si + 1)
        gate_v.clear()
        p.wait("sync", "s", attn_s)
        emit_proj_dmats(si)
        for u in range(NSUB):
            emit_projsub(si, u)
            if u >= 1:
                emit_gate_mm(si, u - 1)
        emit_gate_mm(si, NSUB - 1)
        for u in range(NSUB):
            emit_gate_fin(si, u)
        p.wait("scalar", "v", gate_v[-1])
        p.dma("scalar", nm_hbm[:, :], notT[:, :])
        nm_write_d[0] = p.cnt["dA"]
        p.wait("scalar", "dA")
        p.wait("scalar", "t", nm_guard_t[0])
        p.dma("scalar", nmflat[0:1, :], bass.AP(nm_hbm, 0, [[1, 1], [1, 8 * SLAB]]))
        d_nm = p.cnt["dA"]
        h_s = {}
        for e_i in range(E):
            p.wait("tensor", "v", ffnw_v)
            p.wait("tensor", "s", ffnw_s)
            p.wait("tensor", "dA", d_nm)
            for hc in range(2):
                bk = hc
                bank_wait(bk)
                for k in range(8):
                    lhs = W1b[:, (e_i * 8 + k) * FH + hc * 128: (e_i * 8 + k) * FH + (hc + 1) * 128]
                    rhs = xh1T[:, k * 512:(k + 1) * 512]
                    p.emit("tensor", lambda e, bk=bk, l=lhs, r=rhs, st=(k == 0):
                           e.matmul(psum[bk][:, :], l, r, start=st, stop=False))
                p.emit("tensor", lambda e, bk=bk, e_i=e_i:
                       e.matmul(psum[bk][:, :], neg30row[:, :],
                                nmflat[0:1, e_i * SLAB:(e_i + 1) * SLAB], start=False, stop=True), inc="t")
                t_h = p.cnt["t"]
                p.wait("scalar", "t", t_h)
                p.emit("scalar", lambda e, bk=bk, e_i=e_i, hc=hc: e.activation(
                    hb[:, e_i * 1024 + hc * 512: e_i * 1024 + (hc + 1) * 512],
                    psum[bk][:, :], AF.Gelu, bias=b1pT[:, e_i * 2 + hc:e_i * 2 + hc + 1]), inc="s")
                bank_guard[bk] = ("s", p.cnt["s"])
            h_s[e_i] = p.cnt["s"]
        nm_guard_t[0] = p.cnt["t"]
        for u in range(NSUB):
            for half in range(2):
                bk = 4 + half
                bank_wait(bk)
                p.wait("tensor", "s", h_s[E - 1])
                p.emit("tensor", lambda e, bk=bk, u=u, half=half:
                       e.matmul(psum[bk][:, :], ohT[:, u * 128:(u + 1) * 128],
                                b2b[:, half * 512:(half + 1) * 512], start=True, stop=False))
                for e_i in range(E):
                    for hc in range(2):
                        lhs = hb[:, e_i * 1024 + hc * 512 + u * 128: e_i * 1024 + hc * 512 + (u + 1) * 128]
                        rhs = W2b[:, (e_i * 2 + hc) * D + half * 512:(e_i * 2 + hc) * D + (half + 1) * 512]
                        last = (e_i == E - 1 and hc == 1)
                        p.emit("tensor", lambda e, bk=bk, l=lhs, r=rhs, sp=last:
                               e.matmul(psum[bk][:, :], l, r, start=False, stop=sp),
                               inc="t" if last else None)
            t_y = p.cnt["t"]
            p.wait("vector", "t", t_y)
            p.emit("vector", lambda e, u=u: e.tensor_tensor(
                r1[:, 0:512], psum[4][:, :], xh1[:, u * 1024:u * 1024 + 512], ALU.add))
            p.emit("vector", lambda e, u=u: e.tensor_tensor(
                r1[:, 512:1024], psum[5][:, :], xh1[:, u * 1024 + 512:(u + 1) * 1024], ALU.add), inc="v")
            v_r2 = p.cnt["v"]
            bank_guard[4] = ("v", v_r2)
            bank_guard[5] = ("v", v_r2)
            p.wait("scalar", "v", v_r2)
            layer_norm(r1[:, :], D, t_y)
            ob = outt[:, :]
            p.wait("vector", "dA", store_d[0])
            p.emit("vector", lambda e, ob=ob: e.tensor_scalar(
                ob, r1[:, :], s_mu[:, :], s_ri[:, :], ALU.subtract, ALU.mult), inc="v")
            v_o = p.cnt["v"]
            p.wait("scalar", "v", v_o)
            p.dma("scalar", y_out[s0 + u * 128:s0 + (u + 1) * 128, :], ob)
            store_d[0] = p.cnt["dA"]
        projT_dS[0] = p.cnt["dS"]
        slab_end_t[0] = p.cnt["t"]

    p.wait("sync", "dA", p.cnt["dA"])
    p.wait("sync", "dS", p.cnt["dS"])

    with nc.Block() as block:
        @block.sync
        def _(eng):
            for f in p.ops["sync"]:
                f(eng)

        @block.tensor
        def _(eng):
            for f in p.ops["tensor"]:
                f(eng)

        @block.vector
        def _(eng):
            for f in p.ops["vector"]:
                f(eng)

        @block.scalar
        def _(eng):
            for f in p.ops["scalar"]:
                f(eng)

        @block.gpsimd
        def _(eng):
            for f in p.ops["gpsimd"]:
                f(eng)

    ctx.close()
    return nc


_NC_CACHE = [None]


def kernel(L0_attn_x, L0_attn_W_attn, L0_attn_W_proj, L0_ffn_Wg, L0_ffn_ln_scale,
           L0_ffn_ln_bias, L0_ffn_W1, L0_ffn_b1, L0_ffn_W2, L0_ffn_b2):
    x = np.ascontiguousarray(np.asarray(L0_attn_x, dtype=np.float32))
    shared = {
        "w_attn": np.ascontiguousarray(np.asarray(L0_attn_W_attn, dtype=np.float32)),
        "w_proj": np.ascontiguousarray(np.asarray(L0_attn_W_proj, dtype=np.float32)),
        "wg_in": np.ascontiguousarray(np.asarray(L0_ffn_Wg, dtype=np.float32)),
        "lns_in": np.ascontiguousarray(np.asarray(L0_ffn_ln_scale, dtype=np.float32)),
        "lnb_in": np.ascontiguousarray(np.asarray(L0_ffn_ln_bias, dtype=np.float32)),
        "w1_in": np.ascontiguousarray(np.asarray(L0_ffn_W1, dtype=np.float32)),
        "b1_in": np.ascontiguousarray(np.asarray(L0_ffn_b1, dtype=np.float32)),
        "w2_in": np.ascontiguousarray(np.asarray(L0_ffn_W2, dtype=np.float32)),
        "b2_in": np.ascontiguousarray(np.asarray(L0_ffn_b2, dtype=np.float32)),
    }
    if _NC_CACHE[0] is None:
        nc = bacc.Bacc("TRN2", target_bir_lowering=False)
        build(nc)
        nc.finalize()
        _NC_CACHE[0] = nc
    nc = _NC_CACHE[0]
    in_maps = [dict(shared, x_in=np.ascontiguousarray(x[b])) for b in range(8)]
    res = run_bass_kernel_spmd(nc, in_maps, core_ids=list(range(8)))
    return np.stack([res.results[b]["y_out"] for b in range(8)], axis=0)
